# version 15
# baseline (speedup 1.0000x reference)
import numpy as np
import concourse.bass as bass
import concourse.mybir as mybir
from concourse.bass_utils import run_bass_kernel_spmd
from contextlib import ExitStack

F32 = mybir.dt.float32
BF16 = mybir.dt.bfloat16
AF = mybir.ActivationFunctionType
ALU = mybir.AluOpType
AX = mybir.AxisListType

S = 4096
E = 1024
NC = 8
NQT = 32
NBK = 8
FF = 2048
NFF = 16
NEXP = 9
NPAIR = 8

_P_STREAMS = ("wl_pro", "wl_wkv", "wl_xb", "wl_wq", "wl_qkv2", "wl_out",
              "wl_w0", "wl_w1", "wl_b2", "ga")
_Y_STREAMS = ("ld_pro", "ld_x0", "ld_x1t0", "ld_x1t1", "ld_pair0", "ld_pair1",
              "ld_2c", "ld_x2t0", "ld_x2t1", "ld_b1c", "ld_acc",
              "st_x1", "st_x1t", "st_qk", "st_v2", "st_ot", "st_2c", "st_out")
_DMA_ENG = {}
for _s in _P_STREAMS:
    _DMA_ENG[_s] = "P"
for _s in _Y_STREAMS:
    _DMA_ENG[_s] = "Y"


class _Ctx:
    pass


class _Walk:
    def __init__(self, c, eng, e):
        self.c = c
        self.eng = eng
        self.e = e
        self.cnt = {k: 0 for k in c.semnames}
        self.res = {}
        self.tag = {}

    def inc(self, sem, ins=None, n=1):
        self.cnt[sem] += n
        if ins is not None:
            ins.then_inc(self.c.sem[sem], n)
        return self.cnt[sem]

    def wait(self, sem, val):
        if val > 0:
            self.e.wait_ge(self.c.sem[sem], val)

    def dma(self, sem, out, in_, accum=None):
        ins = None
        if self.eng == _DMA_ENG[sem]:
            if accum is None:
                ins = self.e.dma_start(out=out, in_=in_)
            else:
                ins = self.e.dma_start(out=out, in_=in_, accum_op=accum)
        return self.inc(sem, ins, 16)

    def res_wait(self, key):
        sv = self.res.get(key)
        if sv is not None:
            self.wait(sv[0], sv[1])

    def res_set(self, key, sem, val=None):
        self.res[key] = (sem, self.cnt[sem] if val is None else val)

    def bar(self):
        if self.eng == "Y":
            for s in _Y_STREAMS:
                self.wait(s, self.cnt[s])
        if self.eng == "P":
            for s in _P_STREAMS:
                self.wait(s, self.cnt[s])
        self.cnt["bar"] += 5
        self.e.drain()
        self.e.sem_inc(self.c.sem["bar"], 1)
        self.e.wait_ge(self.c.sem["bar"], self.cnt["bar"])


def _build(taps=()):
    nc = bass.Bass("TRN2", target_bir_lowering=False)
    c = _Ctx()
    c.nc = nc
    dt = lambda n, sh, d, k: nc.dram_tensor(n, sh, d, kind=k).ap()
    I, O = "ExternalInput", "ExternalOutput"
    TI = lambda n: O if n in taps else "Internal"

    c.x0 = dt("x0", [S, E], F32, I)
    c.x0T = dt("x0T", [E, S], F32, I)
    c.wqkv1 = dt("wqkv1", [E, 3 * E], F32, I)
    c.wqkv2 = dt("wqkv2", [E, 3 * E], F32, I)
    c.wout = dt("wout", [E, E], F32, I)
    c.sw1 = dt("sw1", [E, FF], F32, I)
    c.sw2 = dt("sw2", [FF, E], F32, I)
    c.rw1 = dt("rw1", [8, E, FF], F32, I)
    c.rw2 = dt("rw2", [8, FF, E], F32, I)
    c.qkcols = dt("qkcols", [128, 16], F32, I)
    c.vb_row = dt("vb_row", [1, E], F32, I)
    c.outb_row = dt("outb_row", [1, E], F32, I)
    c.n1w_row = dt("n1w_row", [1, E], F32, I)
    c.n1b_row = dt("n1b_row", [1, E], F32, I)
    c.n2w_row = dt("n2w_row", [1, E], F32, I)
    c.n2b_row = dt("n2b_row", [1, E], F32, I)
    c.sb1c = dt("sb1c", [128, NFF], F32, I)
    c.sb2_row = dt("sb2_row", [1, E], F32, I)
    c.rb1c = dt("rb1c", [8, 128, NFF], F32, I)
    c.rb2_rows = dt("rb2_rows", [8, E], F32, I)
    c.centT = dt("centT", [NC, 128, 8], F32, I)
    c.rbias_row = dt("rbias_row", [1, 8], F32, I)
    c.ident = dt("ident", [128, 128], F32, I)
    c.cmask = dt("cmask", [128, 128], F32, I)
    c.cmaskT = dt("cmaskT", [128, 128], F32, I)
    c.hmaskc = dt("hmaskc", [128, 128], F32, I)
    c.mrow_hbm = dt("mrow_hbm", [16, 1], F32, "Internal")

    c.x1_hbm = dt("x1_hbm", [S, E], F32, TI("x1_hbm"))
    c.x1T_hbm = dt("x1T_hbm", [NC, 128, S], BF16, TI("x1T_hbm"))
    c.q2T_hbm = dt("q2T_hbm", [NPAIR, 128, S], BF16, TI("q2T_hbm"))
    c.k2T_hbm = dt("k2T_hbm", [NPAIR, 128, S], BF16, TI("k2T_hbm"))
    c.v2_hbm = dt("v2_hbm", [NPAIR, S, 128], BF16, TI("v2_hbm"))
    c.oT_hbm = dt("oT_hbm", [NPAIR, 128, S], BF16, TI("oT_hbm"))
    c.acc_hbm = dt("acc_hbm", [S, E], F32, TI("acc_hbm"))
    c.x2T_hbm = dt("x2T_hbm", [NC, 128, S], BF16, TI("x2T_hbm"))
    c.out = dt("out", [S, E], F32, O)

    with ExitStack() as st:
        en = st.enter_context
        sb = lambda n, sh, d: en(nc.sbuf_tensor(n, sh, d))
        c.identbf = sb("identbf", [128, 128], BF16)
        c.cmask_s = sb("cmask_s", [128, 128], F32)
        c.cmaskT_s = sb("cmaskT_s", [128, 128], F32)
        c.hmask_s = sb("hmask_s", [128, 128], BF16)
        c.mbias = sb("mbias", [128, 16], F32)
        c.mrow_s = sb("mrow_s", [1, 16], F32)
        c.onesbf = sb("onesbf", [1, 128], BF16)
        c.onescolb = sb("onescolb", [128, 1], BF16)
        c.onesf = sb("onesf", [1, 128], F32)
        c.ones2 = sb("ones2", [128, 64], F32)
        c.epsT = sb("epsT", [128, 1], F32)
        c.qk_s = sb("qk_s", [128, 16], F32)
        c.qb8 = sb("qb8", [128, 8], F32)
        c.rbias_b = sb("rbias_b", [128, 8], F32)
        c.nw_b = sb("nw_b", [128, E], F32)
        c.nb_b = sb("nb_b", [128, E], F32)
        c.gates = sb("gates", [128, NQT * 8], F32)
        c.stat = sb("stat", [128, 60], F32)
        c.st6 = sb("st6", [128, 12], F32)
        c.row_a = sb("row_a", [1, E], BF16)
        c.row_c = sb("row_c", [1, E], BF16)
        c.row_f = sb("row_f", [1, E], F32)
        c.row_r = sb("row_r", [1, 8], F32)
        c.b1c = sb("b1c", [128, NFF], F32)
        c.centbf = sb("centbf", [128, 64], BF16)
        c.AR0 = sb("AR0", [128, 32768], BF16)
        c.AR1 = sb("AR1", [128, 32768], BF16)
        c.AR2 = sb("AR2", [128, 16384], BF16)
        c.AR3 = sb("AR3", [128, 6144], BF16)
        c.ARF = sb("ARF", [128, 3072], F32)
        c.pb01 = en(nc.psum_tensor("pb01", [128, 1024], F32))
        c.pb23 = en(nc.psum_tensor("pb23", [128, 1024], F32))
        c.pb = [c.pb01[:, 0:512], c.pb01[:, 512:1024],
                c.pb23[:, 0:512], c.pb23[:, 512:1024]]
        c.pt = [en(nc.psum_tensor(f"pt{i}", [128, 1024], BF16)) for i in range(1)]
        c.pa = [en(nc.psum_tensor(f"pa{i}", [128, 512], F32)) for i in range(2)]
        c.pr = en(nc.psum_tensor("pr", [128, 512], F32))
        c.semnames = ("bar", "mm", "ev", "sc") + _P_STREAMS + _Y_STREAMS
        c.sem = {n: en(nc.semaphore(n)) for n in c.semnames}

        blk = en(nc.Block())

        def run(eng):
            def f(e):
                _emit(_Walk(c, eng, e))
            return f

        blk.tensor(run("T"))
        blk.vector(run("V"))
        blk.scalar(run("S"))
        blk.gpsimd(run("P"))
        blk.sync(run("Y"))
    return nc, c




def _ln_stats(w, src, n_free):
    c = w.c
    nchunk = (n_free + 511) // 512
    ins = None
    if w.eng == "V":
        for i in range(nchunk):
            a = i * 512
            w.e.bn_stats(c.st6[:, 6 * i:6 * i + 6], src[:, a:min(n_free, a + 512)])
        w.e.drain()
        w.e.bn_aggr(c.stat[:, 2:4], c.st6[:, 0:6 * nchunk])
        ins = w.e.drain()
    vstat = w.inc("ev", ins)
    ins = None
    if w.eng == "S":
        w.wait("ev", vstat)
        w.e.activation(c.stat[:, 4:5], c.stat[:, 3:4], AF.Ln, bias=c.epsT[:, 0:1])
        w.e.drain()
        ins = w.e.activation(c.stat[:, 1:2], c.stat[:, 4:5], AF.Exp, scale=-0.5)
    sdone = w.inc("sc", ins)
    if w.eng == "V":
        w.wait("sc", sdone)


def _bcast_row(w, row_ap, dst, n, pbank, bank_key):
    c = w.c
    for i in range((n + 511) // 512):
        a = i * 512
        b = min(n, a + 512)
        ins = None
        if w.eng == "T":
            w.res_wait(bank_key)
            ins = w.e.matmul(pbank[:, 0:b - a], c.onesf[:, 0:128], row_ap[:, a:b],
                             start=True, stop=True)
        v = w.inc("mm", ins)
        ins = None
        if w.eng == "V":
            w.wait("mm", v)
            ins = w.e.tensor_copy(dst[:, a:b], pbank[:, 0:b - a])
        w.inc("ev", ins)
        w.res_set(bank_key, "ev")


def _transpose_grp(w, buf, soff, dbuf, doff, ntiles, src_ready, dst_ready_key=None):
    c = w.c
    tiles = list(range(ntiles))
    ev_last = w.cnt["ev"]
    for gi in range(0, len(tiles), 4):
        g = tiles[gi:gi + 4]
        bank = (gi // 4) % len(c.pt)
        pt = c.pt[bank]
        ins = None
        if w.eng == "T":
            w.res_wait(f"pt{bank}")
            if gi == 0 and src_ready is not None:
                w.wait(src_ready[0], src_ready[1])
            for j, t in enumerate(g):
                ins = w.e.transpose(pt[:, j * 128:(j + 1) * 128],
                                    buf[:, soff + t * 128:soff + (t + 1) * 128],
                                    c.identbf[:])
        mmv = w.inc("mm", ins)
        ins = None
        if w.eng == "V":
            w.wait("mm", mmv)
            if gi == 0 and dst_ready_key is not None:
                w.res_wait(dst_ready_key)
            a = g[0] * 128
            b = (g[-1] + 1) * 128
            ins = w.e.tensor_copy(dbuf[:, doff + a:doff + b], pt[:, 0:b - a])
        ev_last = w.inc("ev", ins)
        w.res_set(f"pt{bank}", "ev")
    return ev_last




_NSTAGES = 8
_APART = 9


def _emit(w):
    stages = [_prologue, _stageA_kv, _stageA_attn, _stageB_qkv, _stageB_attn,
              _stageB_proj, _stageC, _stageD]
    for i, f in enumerate(stages[:_NSTAGES]):
        f(w)
        if i < len(stages) - 1 and stages[i + 1] is not _stageC:
            w.bar()


def _prologue(w):
    c = w.c
    w.dma("wl_pro", c.identbf[:], c.ident)
    w.dma("wl_pro", c.row_a[:], c.vb_row)
    w.dma("wl_pro", c.hmask_s[:], c.hmaskc)
    w.dma("wl_pro", c.centbf[:].rearrange("p (a f) -> p a f", a=8),
          c.centT.rearrange("c p f -> p c f"))
    wl = w.cnt["wl_pro"]
    w.dma("ld_pro", c.cmask_s[:], c.cmask)
    w.dma("ld_pro", c.cmaskT_s[:], c.cmaskT)
    w.dma("ld_pro", c.qk_s[:], c.qkcols)
    w.dma("ld_pro", c.row_r[:], c.rbias_row)
    w.dma("ld_pro", c.row_f[:], c.n1w_row)
    yl = w.cnt["ld_pro"]
    ins = None
    if w.eng == "V":
        w.wait("ld_pro", yl)
        w.wait("wl_pro", wl)
        w.e.memset(c.epsT[:], 1e-5)
        w.e.memset(c.onesbf[:], 1.0)
        w.e.memset(c.onescolb[:], 1.0)
        w.e.memset(c.onesf[:], 1.0)
        w.e.memset(c.ones2[:], 1.0)
        w.e.tensor_scalar(c.qb8[:], c.qk_s[:, 0:8], 0.125, None, ALU.mult)
        ins = w.e.drain()
    mv = w.inc("ev", ins)
    if w.eng == "T":
        w.wait("ev", mv)
    _bcast_row(w, c.row_r, c.rbias_b, 8, c.pb[0], "pb0")
    _bcast_row(w, c.row_f, c.nw_b, E, c.pb[1], "pb1")
    w.res_set("rowf", "mm")
    if w.eng == "Y":
        w.res_wait("rowf")
    nb_l = w.dma("ld_pro", c.row_f[:], c.n1b_row)
    if w.eng == "T":
        w.wait("ld_pro", nb_l)
    _bcast_row(w, c.row_f, c.nb_b, E, c.pb[2], "pb2")




def _attn_sweep1(w, qt, cfg, par):
    c = w.c
    span = (qt + 1) * 128
    nch = (span + 511) // 512
    heads = cfg["heads"]
    for ch in range(nch):
        cw = min(512, span - ch * 512)
        for h in heads:
            sb = h["sb"][par]
            bank = h["banks"][ch % 2]
            ins = None
            if w.eng == "T":
                if ch == 0 and cfg.get("qk_ready") is not None:
                    w.wait(cfg["qk_ready"][0], cfg["qk_ready"][1])
                w.res_wait(bank[1])
                nci = h["nci"]
                for ci in range(nci):
                    ins = w.e.matmul(bank[0][:, 0:cw], h["lhsT"](ci, qt),
                                     h["rhs"](ci, ch * 512, cw),
                                     start=(ci == 0), stop=(ci == nci - 1),
                                     **h["mmkw"])
            mmv = w.inc("mm", ins)
            ins = None
            if w.eng == "V":
                w.wait("mm", mmv)
                if ch == nch - 1:
                    w.e.tensor_tensor(bank[0][:, cw - 128:cw], bank[0][:, cw - 128:cw],
                                      c.cmask_s[:], ALU.add)
                    w.e.drain()
                ins = w.e.reduce_max(c.stat[:, sb + ch:sb + ch + 1],
                                     bank[0][:, 0:cw], axis=AX.X)
            w.inc("ev", ins)
            w.res_set(bank[1], "ev")
    ins = None
    if w.eng == "V":
        w.e.drain()
        for h in heads:
            sb = h["sb"][par]
            w.e.reduce_max(c.stat[:, sb + 8:sb + 9], c.stat[:, sb:sb + nch], axis=AX.X)
        w.e.drain()
        for h in heads:
            sb = h["sb"][par]
            w.e.tensor_scalar(c.stat[:, sb + 9:sb + 10], c.stat[:, sb + 8:sb + 9],
                              -1.0, None, ALU.mult)
        ins = w.e.drain()
    return w.inc("ev", ins)


def _attn_sweep2(w, qt, cfg, par, mneg_v):
    c = w.c
    span = (qt + 1) * 128
    nch = (span + 511) // 512
    heads = cfg["heads"]
    for ch in range(nch):
        cw = min(512, span - ch * 512)
        for h in heads:
            sb = h["sb"][par]
            poff = h["poff"](par)
            pbuf = h["pbuf"]
            bank = h["banks"][ch % 2]
            ins = None
            if w.eng == "T":
                if ch == 0 and cfg.get("qk_ready") is not None:
                    w.wait(cfg["qk_ready"][0], cfg["qk_ready"][1])
                w.res_wait(bank[1])
                nci = h["nci"]
                for ci in range(nci):
                    ins = w.e.matmul(bank[0][:, 0:cw], h["lhsT"](ci, qt),
                                     h["rhs"](ci, ch * 512, cw),
                                     start=(ci == 0), stop=(ci == nci - 1),
                                     **h["mmkw"])
            mmv = w.inc("mm", ins)
            if ch == nch - 1:
                ins = None
                if w.eng == "V":
                    w.wait("mm", mmv)
                    ins = w.e.tensor_tensor(bank[0][:, cw - 128:cw],
                                            bank[0][:, cw - 128:cw],
                                            c.cmask_s[:], ALU.add)
                mk = w.inc("ev", ins)
            else:
                mk = w.cnt["ev"]
            ins = None
            if w.eng == "S":
                w.wait("mm", mmv)
                w.wait("ev", max(mneg_v, mk))
                if ch == 0:
                    w.res_wait(h["pkey"] + str(par))
                ins = w.e.activation(pbuf[:, poff + ch * 512:poff + ch * 512 + cw],
                                     bank[0][:, 0:cw], AF.Exp,
                                     bias=c.stat[:, sb + 9:sb + 10],
                                     accum_out=c.stat[:, sb + 10 + ch:sb + 11 + ch])
            w.inc("sc", ins)
            w.res_set(bank[1], "sc")
    exp_v = w.cnt["sc"]
    ins = None
    if w.eng == "V":
        w.wait("sc", exp_v)
        for h in heads:
            sb = h["sb"][par]
            w.e.reduce_sum(c.stat[:, sb + 18:sb + 19], c.stat[:, sb + 10:sb + 10 + nch],
                           axis=AX.X)
        w.e.drain()
        for h in heads:
            sb = h["sb"][par]
            w.e.reciprocal(c.stat[:, sb + 19:sb + 20], c.stat[:, sb + 18:sb + 19])
        w.e.drain()
        for h in heads:
            sb = h["sb"][par]
            poff = h["poff"](par)
            w.e.tensor_scalar(h["pbuf"][:, poff:poff + span],
                              h["pbuf"][:, poff:poff + span],
                              c.stat[:, sb + 19:sb + 20], None, ALU.mult)
        ins = w.e.drain()
    return w.inc("ev", ins)


def _attn_transpose(w, qt, cfg, par, norm_v, evac_eng):
    c = w.c
    span = (qt + 1) * 128
    out = []
    for hi, h in enumerate(cfg["heads"]):
        poff = h["poff"](par)
        pbuf = h["pbuf"]
        eng = evac_eng[hi % len(evac_eng)]
        ntiles = span // 128
        ev_last = (("ev", w.cnt["ev"]) if eng == "V" else ("sc", w.cnt["sc"]))
        for gi in range(0, ntiles, 4):
            g = list(range(gi, min(gi + 4, ntiles)))
            banki = (gi // 4) % len(c.pt)
            pt = c.pt[banki]
            ins = None
            if w.eng == "T":
                w.res_wait(f"pt{banki}")
                if gi == 0:
                    w.wait("ev", norm_v)
                for t in g:
                    ins = w.e.transpose(pt[:, (t - gi) * 128:(t - gi + 1) * 128],
                                        pbuf[:, poff + t * 128:poff + (t + 1) * 128],
                                        c.identbf[:])
            mmv = w.inc("mm", ins)
            a, b = g[0] * 128, (g[-1] + 1) * 128
            ins = None
            if w.eng == eng:
                w.wait("mm", mmv)
                if eng == "V":
                    ins = w.e.tensor_copy(pbuf[:, poff + a:poff + b], pt[:, 0:b - a])
                else:
                    ins = w.e.activation(pbuf[:, poff + a:poff + b], pt[:, 0:b - a],
                                         AF.Copy)
            sem = "ev" if eng == "V" else "sc"
            ev_last = (sem, w.inc(sem, ins))
            w.res_set(f"pt{banki}", sem)
        out.append(ev_last)
    return out


def _stageA(w):
    _stageA_kv(w)
    w.bar()
    _stageA_attn(w)


def _stageA_kv(w):
    c = w.c
    KT, V = c.AR0, c.AR1

    for ci in range(NC):
        w.dma("wl_wkv", c.AR2[:, ci * 2048:(ci + 1) * 2048],
              c.wqkv1[ci * 128:(ci + 1) * 128, E:3 * E])
    wkv_l = w.cnt["wl_wkv"]

    for b in range(NBK):
        bs = b * 512
        if w.eng == "P":
            w.res_wait("xb")
        xb_l = w.dma("wl_xb", c.AR3[:, 0:4096].rearrange("p (c t) -> p c t", c=8),
                     c.x0T[:, bs:bs + 512].rearrange("(c p) t -> p c t", p=128))
        for nco in range(NC):
            pbk = c.pb[nco % 4]
            ins = None
            if w.eng == "T":
                w.wait("wl_wkv", wkv_l)
                w.wait("wl_xb", xb_l)
                w.res_wait(f"pb{nco % 4}")
                for ci in range(NC):
                    ins = w.e.matmul(pbk[:],
                                     c.AR2[:, ci * 2048 + nco * 128:ci * 2048 + nco * 128 + 128],
                                     c.AR3[:, ci * 512:(ci + 1) * 512],
                                     start=(ci == 0), stop=(ci == NC - 1))
            mmv = w.inc("mm", ins)
            ins = None
            if w.eng == "S":
                w.wait("mm", mmv)
                ins = w.e.activation(KT[:, nco * 4096 + bs:nco * 4096 + bs + 512],
                                     pbk[:], AF.Copy)
            w.inc("sc", ins)
            w.res_set(f"pb{nco % 4}", "sc")
        for tt in range(4):
            for nb2 in range(2):
                bk = (tt * 2 + nb2) % 4
                pbk = c.pb[bk]
                ins = None
                if w.eng == "T":
                    w.res_wait(f"pb{bk}")
                    for ci in range(NC):
                        ins = w.e.matmul(pbk[:],
                                         c.AR3[:, ci * 512 + tt * 128:ci * 512 + tt * 128 + 128],
                                         c.AR2[:, ci * 2048 + 1024 + nb2 * 512:ci * 2048 + 1024 + (nb2 + 1) * 512],
                                         start=(ci == 0), stop=(ci == NC - 1))
                mmv = w.inc("mm", ins)
                ins = None
                if w.eng == "V":
                    w.wait("mm", mmv)
                    ins = w.e.tensor_copy(
                        V[:, (b * 4 + tt) * 1024 + nb2 * 512:(b * 4 + tt) * 1024 + (nb2 + 1) * 512],
                        pbk[:])
                w.inc("ev", ins)
                w.res_set(f"pb{bk}", "ev")
        w.res_set("xb", "mm")


def _stageA_attn(w):
    c = w.c
    KT, V = c.AR0, c.AR1
    for ci in range(NC):
        w.dma("wl_wq", c.AR2[:, ci * 1024:(ci + 1) * 1024],
              c.wqkv1[ci * 128:(ci + 1) * 128, 0:E])
    wq_l = w.cnt["wl_wq"]

    for b in range(NBK):
        for nco in range(NC):
            sl = (b * NC + nco) % 2
            scr = c.AR3[:, sl * 512:(sl + 1) * 512]
            ins = None
            if w.eng == "V":
                w.res_wait(f"ksq{sl}")
                src = KT[:, nco * 4096 + b * 512:nco * 4096 + (b + 1) * 512]
                ins = w.e.tensor_tensor(scr, src, src, ALU.mult)
            kv = w.inc("ev", ins)
            ins = None
            if w.eng == "T":
                w.wait("ev", kv)
                if nco == 0:
                    w.res_wait("prq")
                ins = w.e.matmul(c.pr[0:1, 0:512], c.onescolb[:, 0:1], scr,
                                 start=(nco == 0), stop=(nco == NC - 1))
            w.inc("mm", ins)
            w.res_set(f"ksq{sl}", "mm")
        kmm = w.cnt["mm"]
        ins = None
        if w.eng == "V":
            w.wait("mm", kmm)
            ins = w.e.reduce_max(c.stat[0:1, 16 + b:17 + b], c.pr[0:1, 0:512],
                                 axis=AX.X)
        w.inc("ev", ins)
        w.res_set("prq", "ev")
    ins = None
    if w.eng == "V":
        w.e.drain()
        ins = w.e.reduce_max(c.stat[0:1, 24:25], c.stat[0:1, 16:24], axis=AX.X)
    gkv = w.inc("ev", ins)
    ins = None
    if w.eng == "T":
        w.wait("ev", gkv)
        w.res_wait("pb0")
        ins = w.e.matmul(c.pb[0][0:128, 0:1], c.onesf[0:1, 0:128],
                         c.stat[0:1, 24:25], start=True, stop=True)
    bkm = w.inc("mm", ins)
    ins = None
    if w.eng == "V":
        w.wait("mm", bkm)
        ins = w.e.tensor_copy(c.stat[:, 58:59], c.pb[0][:, 0:1])
    w.inc("ev", ins)
    w.res_set("pb0", "ev")

    for qb in range(NBK):
        bs = qb * 512
        if w.eng == "P":
            w.res_wait("xb")
        xb_l = w.dma("wl_xb", c.AR2[:, 12288:16384].rearrange("p (c t) -> p c t", c=8),
                     c.x0T[:, bs:bs + 512].rearrange("(c p) t -> p c t", p=128))
        for nco in range(NC):
            pbk = c.pb[nco % 4]
            ins = None
            if w.eng == "T":
                w.wait("wl_wq", wq_l)
                w.wait("wl_xb", xb_l)
                w.res_wait(f"pb{nco % 4}")
                if nco == 0:
                    w.res_wait("qtbuf")
                for ci in range(NC):
                    ins = w.e.matmul(pbk[:],
                                     c.AR2[:, ci * 1024 + nco * 128:ci * 1024 + nco * 128 + 128],
                                     c.AR2[:, 12288 + ci * 512:12288 + (ci + 1) * 512],
                                     start=(ci == 0), stop=(ci == NC - 1))
            mmv = w.inc("mm", ins)
            ins = None
            if w.eng == "S":
                w.wait("mm", mmv)
                ins = w.e.activation(c.AR2[:, 8192 + nco * 512:8192 + (nco + 1) * 512],
                                     pbk[:], AF.Copy, scale=1.0 / 32.0)
            w.inc("sc", ins)
            w.res_set(f"pb{nco % 4}", "sc")
        qt_ready = ("sc", w.cnt["sc"])
        w.res_set("xb", "mm")

        for ql in range(4):
            qt = qb * 4 + ql
            if w.eng == "Y":
                w.res_wait("x0t")
            x0_l = w.dma("ld_x0", c.ARF[:, 0:1024], c.x0[qt * 128:(qt + 1) * 128, :])

            heads = [dict(
                lhsT=lambda ci, q, _ql=ql: c.AR2[:, 8192 + ci * 512 + _ql * 128:8192 + ci * 512 + _ql * 128 + 128],
                rhs=lambda ci, ks, kw: KT[:, ci * 4096 + ks:ci * 4096 + ks + kw],
                nci=NC, mmkw={}, banks=[(c.pb[0], "pb0"), (c.pb[1], "pb1")],
                sb={0: 8}, poff=lambda par: 0, pbuf=c.AR3, pkey="PA",
            )]
            cfg = {"heads": heads, "qk_ready": qt_ready}
            span = (qt + 1) * 128

            ins = None
            if w.eng == "V":
                w.res_wait("x1b")
                w.wait("sc", qt_ready[1])
                for ci in range(NC):
                    src = c.AR2[:, 8192 + ci * 512 + ql * 128:8192 + ci * 512 + ql * 128 + 128]
                    w.e.tensor_tensor(c.AR3[:, 4096 + ci * 128:4096 + (ci + 1) * 128],
                                      src, src, ALU.mult)
                ins = w.e.drain()
            sqv = w.inc("ev", ins)
            ins = None
            if w.eng == "T":
                w.wait("ev", sqv)
                w.res_wait("prq")
                for ci in range(NC):
                    ins = w.e.matmul(c.pr[0:128, 0:1],
                                     c.AR3[:, 4096 + ci * 128:4096 + (ci + 1) * 128],
                                     c.onescolb[:, 0:1],
                                     start=(ci == 0), stop=(ci == NC - 1))
            qmm = w.inc("mm", ins)
            w.res_set("x1b", "mm")
            ins = None
            if w.eng == "V":
                w.wait("mm", qmm)
                ins = w.e.tensor_tensor(c.stat[:, 50:51], c.pr[:, 0:1],
                                        c.stat[:, 58:59], ALU.mult)
            q2v = w.inc("ev", ins)
            w.res_set("prq", "ev")
            ins = None
            if w.eng == "S":
                w.wait("ev", q2v)
                w.e.activation(c.stat[:, 51:52], c.stat[:, 50:51], AF.Ln)
                w.e.drain()
                ins = w.e.activation(c.stat[:, 52:53], c.stat[:, 51:52],
                                     AF.Exp, scale=0.5)
            qsv = w.inc("sc", ins)
            ins = None
            if w.eng == "V":
                w.wait("sc", qsv)
                ins = w.e.tensor_scalar(c.stat[:, 17:18], c.stat[:, 52:53],
                                        -1.0, None, ALU.mult)
            mneg_v = w.inc("ev", ins)

            norm_v = _attn_sweep2(w, qt, cfg, 0, mneg_v)
            evs = _attn_transpose(w, qt, cfg, 0, norm_v, ["V"])
            ev = evs[0]

            ins = None
            if w.eng == "T":
                w.wait(ev[0], ev[1])
                w.res_wait("pa0")
                w.res_wait("pa1")
                last = span // 128 - 1
                for kt in range(span // 128):
                    w.e.matmul(c.pa[0][:], c.AR3[:, kt * 128:(kt + 1) * 128],
                               V[:, kt * 1024:kt * 1024 + 512],
                               start=(kt == 0), stop=(kt == last))
                    ins = w.e.matmul(c.pa[1][:], c.AR3[:, kt * 128:(kt + 1) * 128],
                                     V[:, kt * 1024 + 512:(kt + 1) * 1024],
                                     start=(kt == 0), stop=(kt == last))
            av_v = w.inc("mm", ins)
            w.res_set("PA0", "mm")

            ins = None
            if w.eng == "V":
                w.wait("mm", av_v)
                w.wait("ld_x0", x0_l)
                w.res_wait("tf")
                w.e.tensor_tensor(c.ARF[:, 1024:1536], c.pa[0][:], c.ARF[:, 0:512], ALU.add)
                w.e.tensor_tensor(c.ARF[:, 1536:2048], c.pa[1][:], c.ARF[:, 512:1024], ALU.add)
                ins = w.e.drain()
            w.inc("ev", ins)
            w.res_set("pa0", "ev")
            w.res_set("pa1", "ev")
            w.res_set("x0t", "ev")
            _ln_stats(w, c.ARF[:, 1024:2048], 1024)
            ins = None
            if w.eng == "V":
                w.res_wait("x1f")
                w.e.tensor_scalar(c.ARF[:, 2048:3072], c.ARF[:, 1024:2048],
                                  c.stat[:, 2:3], c.stat[:, 1:2], ALU.subtract, ALU.mult)
                w.e.drain()
                w.res_wait("x1b")
                w.e.tensor_copy(c.AR3[:, 4096:5120], c.ARF[:, 2048:3072])
                ins = w.e.drain()
            x1_v = w.inc("ev", ins)
            w.res_set("tf", "ev")
            if w.eng == "Y":
                w.wait("ev", x1_v)
            w.dma("st_x1", c.x1_hbm[qt * 128:(qt + 1) * 128, :], c.ARF[:, 2048:3072])
            w.res_set("x1f", "st_x1")
            x1t_ev = _transpose_grp(w, c.AR3, 4096, c.AR3, 5120, 8, ("ev", x1_v),
                                    dst_ready_key="x1t")
            w.res_set("x1b", "mm")
            if w.eng == "Y":
                w.wait("ev", x1t_ev)
            w.dma("st_x1t",
                  c.x1T_hbm[:, :, qt * 128:(qt + 1) * 128].rearrange("c p t -> p c t"),
                  c.AR3[:, 5120:6144].rearrange("p (c t) -> p c t", c=8))
            w.res_set("x1t", "st_x1t")
        w.res_set("qtbuf", "mm")




def _stageB_qkv(w):
    c = w.c
    for ci in range(NC):
        w.dma("wl_qkv2", c.AR0[:, ci * 3072:(ci + 1) * 3072],
              c.wqkv2[ci * 128:(ci + 1) * 128, :])
    wl_v = w.cnt["wl_qkv2"]

    for b in range(NBK):
        bs = b * 512
        xoff = (b % 2) * 4096
        if w.eng == "Y":
            w.res_wait(f"x1tb{b % 2}")
        xst = f"ld_x1t{b % 2}"
        xb_l = w.dma(xst, c.AR1[:, xoff:xoff + 4096].rearrange("p (c t) -> p c t", c=8),
                     c.x1T_hbm[:, :, bs:bs + 512].rearrange("c p t -> p c t"))

        for nco in range(16):
            bk = nco % 4
            pbk = c.pb[bk]
            ins = None
            if w.eng == "T":
                w.wait("wl_qkv2", wl_v)
                w.wait(xst, xb_l)
                w.res_wait(f"pb{bk}")
                for ci in range(NC):
                    ins = w.e.matmul(pbk[:],
                                     c.AR0[:, ci * 3072 + nco * 128:ci * 3072 + nco * 128 + 128],
                                     c.AR1[:, xoff + ci * 512:xoff + (ci + 1) * 512],
                                     start=(ci == 0), stop=(ci == NC - 1))
            mmv = w.inc("mm", ins)
            so = bk * 512
            ins = None
            if w.eng == "V":
                w.wait("mm", mmv)
                w.res_wait(f"qks{bk}")
                if nco < 8:
                    ins = w.e.tensor_scalar(c.AR3[:, so:so + 512], pbk[:], 0.125,
                                            c.qb8[:, nco:nco + 1], ALU.mult, ALU.add)
                else:
                    ins = w.e.tensor_scalar(c.AR3[:, so:so + 512], pbk[:],
                                            c.qk_s[:, nco:nco + 1], None, ALU.add)
            scv = w.inc("ev", ins)
            w.res_set(f"pb{bk}", "ev")
            if w.eng == "Y":
                w.wait("ev", scv)
            tgt = c.q2T_hbm if nco < 8 else c.k2T_hbm
            w.dma("st_qk", tgt[nco % 8, :, bs:bs + 512], c.AR3[:, so:so + 512])
            w.res_set(f"qks{bk}", "st_qk")

            sqo = 3072 + (nco % 2) * 512
            ci = nco % 8
            ins = None
            if w.eng == "V":
                w.res_wait(f"sq{nco % 2}")
                w.e.tensor_tensor(c.AR3[:, sqo:sqo + 512], c.AR3[:, so:so + 512],
                                  c.AR3[:, so:so + 512], ALU.mult)
                ins = w.e.drain()
            sqv = w.inc("ev", ins)
            ins = None
            if w.eng == "T":
                w.wait("ev", sqv)
                if nco in (0, 8):
                    w.res_wait("pa0")
                ins = w.e.matmul(c.pa[0][0:16, 0:512],
                                 c.hmask_s[:, ci * 16:(ci + 1) * 16],
                                 c.AR3[:, sqo:sqo + 512],
                                 start=(nco in (0, 8)), stop=(nco in (7, 15)))
            sqm = w.inc("mm", ins)
            w.res_set(f"sq{nco % 2}", "mm")
            if nco in (7, 15):
                col = (30 + b) if nco == 7 else (38 + b)
                ins = None
                if w.eng == "V":
                    w.wait("mm", sqm)
                    ins = w.e.reduce_max(c.stat[0:16, col:col + 1],
                                         c.pa[0][0:16, 0:512], axis=AX.X)
                w.inc("ev", ins)
                w.res_set("pa0", "ev")
        for tt in range(4):
            for nb2 in range(2):
                bk = (tt * 2 + nb2) % 4
                pbk = c.pb[bk]
                ins = None
                if w.eng == "T":
                    w.res_wait(f"pb{bk}")
                    for ci in range(NC):
                        w.e.matmul(pbk[:],
                                   c.AR1[:, xoff + ci * 512 + tt * 128:xoff + ci * 512 + tt * 128 + 128],
                                   c.AR0[:, ci * 3072 + 2048 + nb2 * 512:ci * 3072 + 2048 + (nb2 + 1) * 512],
                                   start=(ci == 0), stop=False)
                    ins = w.e.matmul(pbk[:], c.onesbf[:, 0:128],
                                     c.row_a[:, nb2 * 512:(nb2 + 1) * 512],
                                     start=False, stop=True)
                mmv = w.inc("mm", ins)
                vo = 2048 + ((tt * 2 + nb2) % 2) * 512
                ins = None
                if w.eng == "V":
                    w.wait("mm", mmv)
                    w.res_wait(f"v2s{(tt * 2 + nb2) % 2}")
                    ins = w.e.tensor_copy(c.AR3[:, vo:vo + 512], pbk[:])
                evv = w.inc("ev", ins)
                w.res_set(f"pb{bk}", "ev")
                if w.eng == "Y":
                    w.wait("ev", evv)
                for j in range(4):
                    w.dma("st_v2", c.v2_hbm[nb2 * 4 + j, bs + tt * 128:bs + (tt + 1) * 128, :],
                          c.AR3[:, vo + j * 128:vo + (j + 1) * 128])
                w.res_set(f"v2s{(tt * 2 + nb2) % 2}", "st_v2")
        w.res_set(f"x1tb{b % 2}", "mm")

    ins = None
    if w.eng == "V":
        w.e.drain()
        w.e.reduce_max(c.stat[0:16, 46:47], c.stat[0:16, 30:38], axis=AX.X)
        w.e.reduce_max(c.stat[0:16, 47:48], c.stat[0:16, 38:46], axis=AX.X)
        w.e.drain()
        w.e.tensor_tensor(c.stat[0:16, 48:49], c.stat[0:16, 46:47],
                          c.stat[0:16, 47:48], ALU.mult)
        ins = w.e.drain()
    mv = w.inc("ev", ins)
    if w.eng == "Y":
        w.wait("ev", mv)
    st1 = w.dma("st_qk", c.mrow_hbm[:, :], c.stat[0:16, 48:49])
    if w.eng == "Y":
        w.wait("st_qk", st1)
    ld1 = w.dma("ld_pro", c.mrow_s[:], c.mrow_hbm.rearrange("p o -> o p"))
    ins = None
    if w.eng == "V":
        w.wait("ld_pro", ld1)
        pairs = c.mrow_s[:, 0:16].rearrange("o (a b) -> o a b", b=2)
        ins = w.e.tensor_tensor(c.row_r[:, 0:8].rearrange("o (a b) -> o a b", b=1),
                                pairs[:, :, 0:1], pairs[:, :, 1:2], ALU.max)
    pv = w.inc("ev", ins)
    ins = None
    if w.eng == "S":
        w.wait("ev", pv)
        ins = w.e.activation(c.row_r[0:1, 0:8], c.row_r[0:1, 0:8], AF.Sqrt)
    sv = w.inc("sc", ins)
    ins = None
    if w.eng == "V":
        w.wait("sc", sv)
        ins = w.e.tensor_scalar(c.row_r[0:1, 0:8], c.row_r[0:1, 0:8],
                                -1.0, None, ALU.mult)
    nv = w.inc("ev", ins)
    ins = None
    if w.eng == "T":
        w.wait("ev", nv)
        w.res_wait("pr")
        ins = w.e.matmul(c.pr[0:128, 0:8], c.onesf[0:1, 0:128],
                         c.row_r[0:1, 0:8], start=True, stop=True)
    bv = w.inc("mm", ins)
    ins = None
    if w.eng == "V":
        w.wait("mm", bv)
        ins = w.e.tensor_copy(c.mbias[:, 0:8], c.pr[:, 0:8])
    w.inc("ev", ins)
    w.res_set("pr", "ev")


def _stageB_attn(w):
    c = w.c
    PT = (8320, 9344)
    OTA, OTB = 10368, 10880
    ps = [c.pb01, c.pb23]
    avA = c.pa[0]
    avB = c.pa[1]
    lbc = c.pr

    ins = None
    if w.eng == "V":
        for s in range(2):
            w.e.memset(c.AR1[:, s * 4160:(s + 1) * 4160], 1.0)
        ins = w.e.drain()
    msv = w.inc("ev", ins)

    def load_pair(hp):
        s = hp % 2
        b0 = s * 8192
        vb = s * 4160
        stream = f"ld_pair{s}"
        if w.eng == "Y":
            if hp < 2:
                w.wait("ev", msv)
            w.res_wait(f"pair{s}")
        w.dma(stream, c.AR0[:, b0:b0 + 4096], c.q2T_hbm[hp, :, :])
        w.dma(stream, c.AR0[:, b0 + 4096:b0 + 8192], c.k2T_hbm[hp, :, :])
        vv = c.AR1[:, vb:vb + 4160].rearrange("p (a f) -> p a f", f=130)
        src = c.v2_hbm[hp, :, :].rearrange("(a p) f -> p a f", p=128)
        w.dma(stream, vv[:, :, 0:64], src[:, :, 0:64])
        return (stream, w.dma(stream, vv[:, :, 65:129], src[:, :, 64:128]))

    pend_epi = [None, None]
    pair_l = load_pair(0)
    for hp in range(NPAIR):
        s = hp % 2
        b0 = s * 8192
        vb = s * 4160
        vv = c.AR1[:, vb:vb + 4160].rearrange("p (a f) -> p a f", f=130)
        nxt = load_pair(hp + 1) if hp + 1 < NPAIR else None
        for qb in range(8):
            ktmax = 4 * qb + 3
            pend_av = None
            for kt in range(ktmax + 1):
                jd = kt - 4 * qb
                off = max(0, jd) * 128
                width = 512 - off
                sl = kt % 2
                psl = ps[sl]
                ins = None
                if w.eng == "T":
                    if qb == 0 and kt == 0:
                        w.wait(pair_l[0], pair_l[1])
                    w.res_wait(f"pscore{sl}")
                    w.e.matmul(psl[:, 0:width],
                               c.AR0[0:64, b0 + 4096 + kt * 128:b0 + 4096 + (kt + 1) * 128],
                               c.AR0[0:64, b0 + qb * 512 + off:b0 + (qb + 1) * 512],
                               start=True, stop=True, tile_position=(0, 0))
                    ins = w.e.matmul(psl[:, 512:512 + width],
                               c.AR0[64:128, b0 + 4096 + kt * 128:b0 + 4096 + (kt + 1) * 128],
                               c.AR0[64:128, b0 + qb * 512 + off:b0 + (qb + 1) * 512],
                               start=True, stop=True, tile_position=(64, 0))
                smm = w.inc("mm", ins)
                if kt == 1 and pend_epi[0] is not None:
                    pend_epi[0]()
                    pend_epi[0] = None
                if kt == 2 and pend_epi[1] is not None:
                    pend_epi[1]()
                    pend_epi[1] = None
                if pend_av is not None:
                    pend_av()
                if jd >= 0:
                    ins = None
                    if w.eng == "V":
                        w.wait("mm", smm)
                        w.e.tensor_tensor(psl[:, 0:128], psl[:, 0:128],
                                          c.cmaskT_s[:], ALU.add)
                        ins = w.e.tensor_tensor(psl[:, 512:640], psl[:, 512:640],
                                                c.cmaskT_s[:], ALU.add)
                    mkv = w.inc("ev", ins)
                else:
                    mkv = None
                ins = None
                if w.eng == "S":
                    w.wait("mm", smm)
                    if mkv is not None:
                        w.wait("ev", mkv)
                    w.res_wait(f"PT{sl}")
                    if jd >= 0:
                        w.e.activation(c.AR1[:, PT[sl]:PT[sl] + width],
                                       psl[:, 0:width], AF.Exp,
                                       bias=c.mbias[:, hp:hp + 1])
                        ins = w.e.activation(c.AR1[:, PT[sl] + 512:PT[sl] + 512 + width],
                                             psl[:, 512:512 + width], AF.Exp,
                                             bias=c.mbias[:, hp:hp + 1])
                    else:
                        ins = w.e.activation(c.AR1[:, PT[sl]:PT[sl] + 1024],
                                             psl[:, 0:1024], AF.Exp,
                                             bias=c.mbias[:, hp:hp + 1])
                exv = w.inc("sc", ins)
                w.res_set(f"pscore{sl}", "sc")

                def av(kt=kt, jd=jd, width=width, sl=sl, exv=exv, vv=vv):
                    first = (kt == 0)
                    ins = None
                    if w.eng == "T":
                        w.wait("sc", exv)
                        if first:
                            w.res_wait("avfree")
                        for h in range(2):
                            bank = avA if h == 0 else avB
                            lhs = vv[:, kt, 65 * h:65 * h + 65]
                            pt_off = PT[sl] + 512 * h
                            if jd < 0:
                                ins = w.e.matmul(bank[0:65, 0:512], lhs,
                                                 c.AR1[:, pt_off:pt_off + 512],
                                                 start=first, stop=False)
                            else:
                                ins = w.e.matmul(bank[0:65, jd * 128:(jd + 1) * 128],
                                                 lhs, c.AR1[:, pt_off:pt_off + 128],
                                                 start=first, stop=True)
                                if jd < 3:
                                    ins = w.e.matmul(bank[0:65, (jd + 1) * 128:512],
                                                     lhs,
                                                     c.AR1[:, pt_off + 128:pt_off + width],
                                                     start=first, stop=False)
                    w.inc("mm", ins)
                    w.res_set(f"PT{sl}", "mm")

                pend_av = av
            pend_av()
            pend_av = None
            av_done = w.cnt["mm"]

            def epi1(av_done=av_done):
                ins = None
                if w.eng == "V":
                    w.wait("mm", av_done)
                    w.res_wait("stgf")
                    w.e.tensor_copy(c.ARF[0:65, 0:512], avA[0:65, 0:512])
                    w.e.tensor_copy(c.ARF[0:65, 512:1024], avB[0:65, 0:512])
                    ins = w.e.drain()
                w.inc("ev", ins)
                w.res_set("avfree", "ev")
                w.tag["rcv"] = w.cnt["ev"]

            def epi2(hp=hp, qb=qb):
                ins = None
                if w.eng == "T":
                    w.wait("ev", w.tag["rcv"])
                    w.res_wait("lbc")
                    w.e.matmul(lbc[0:64, 0:512], c.ones2[64:65, 0:64],
                               c.ARF[64:65, 0:512], start=True, stop=True)
                    ins = w.e.matmul(lbc[64:128, 0:512], c.ones2[64:65, 0:64],
                                     c.ARF[64:65, 512:1024], start=True, stop=True)
                rlv = w.inc("mm", ins)
                ins = None
                if w.eng == "V":
                    w.wait("mm", rlv)
                    w.res_wait("otf")
                    w.e.reciprocal(lbc[:, 0:512], lbc[:, 0:512])
                    w.e.drain()
                    w.e.tensor_tensor(c.AR1[0:64, OTA:OTA + 512],
                                      c.ARF[0:64, 0:512], lbc[0:64, 0:512],
                                      ALU.mult)
                    ins = w.e.tensor_tensor(c.AR1[0:64, OTB:OTB + 512],
                                            c.ARF[0:64, 512:1024],
                                            lbc[64:128, 0:512], ALU.mult)
                mlv = w.inc("ev", ins)
                w.res_set("lbc", "ev")
                w.res_set("stgf", "ev")
                if w.eng == "Y":
                    w.wait("ev", mlv)
                w.dma("st_ot", c.oT_hbm[hp, 0:64, qb * 512:(qb + 1) * 512],
                      c.AR1[0:64, OTA:OTA + 512])
                w.dma("st_ot", c.oT_hbm[hp, 64:128, qb * 512:(qb + 1) * 512],
                      c.AR1[0:64, OTB:OTB + 512])
                w.res_set("otf", "st_ot")

            if pend_epi[0] is not None:
                pend_epi[0]()
            if pend_epi[1] is not None:
                pend_epi[1]()
            pend_epi = [epi1, epi2]
        w.res_set(f"pair{s}", "mm")
        if nxt is not None:
            pair_l = nxt
    pend_epi[0]()
    pend_epi[1]()


def _stageB_proj(w):
    c = w.c
    for ci in range(NC):
        w.dma("wl_out", c.AR1[:, ci * 1024:(ci + 1) * 1024],
              c.wout[ci * 128:(ci + 1) * 128, :])
    w.dma("wl_out", c.row_a[:], c.outb_row)
    wl_v = w.cnt["wl_out"]

    for qt in range(NQT):
        if w.eng == "Y":
            w.res_wait("otb")
        ld1 = w.dma("ld_2c", c.AR3[:, 0:1024].rearrange("p (a t) -> p a t", a=8),
                    c.oT_hbm[:, :, qt * 128:(qt + 1) * 128].rearrange("c p t -> p c t"))
        if w.eng == "Y":
            w.res_wait("x1t2")
        ld2 = w.dma("ld_2c", c.ARF[:, 0:1024], c.x1_hbm[qt * 128:(qt + 1) * 128, :])

        for nb2 in range(2):
            pbk = c.pb[nb2]
            ins = None
            if w.eng == "T":
                w.wait("ld_2c", ld2)
                w.wait("wl_out", wl_v)
                w.res_wait(f"pb{nb2}")
                for ci in range(NC):
                    w.e.matmul(pbk[:],
                               c.AR3[:, ci * 128:(ci + 1) * 128],
                               c.AR1[:, ci * 1024 + nb2 * 512:ci * 1024 + (nb2 + 1) * 512],
                               start=(ci == 0), stop=False)
                ins = w.e.matmul(pbk[:], c.onesbf[:, 0:128],
                                 c.row_a[:, nb2 * 512:(nb2 + 1) * 512],
                                 start=False, stop=True)
            w.inc("mm", ins)
        mm_v = w.cnt["mm"]
        w.res_set("otb", "mm")
        ins = None
        if w.eng == "V":
            w.wait("mm", mm_v)
            w.wait("ld_2c", ld2)
            w.res_wait("tf2")
            w.e.tensor_tensor(c.ARF[:, 1024:1536], c.pb[0][:], c.ARF[:, 0:512], ALU.add)
            w.e.tensor_tensor(c.ARF[:, 1536:2048], c.pb[1][:], c.ARF[:, 512:1024], ALU.add)
            ins = w.e.drain()
        w.inc("ev", ins)
        w.res_set("pb0", "ev")
        w.res_set("pb1", "ev")
        w.res_set("x1t2", "ev")
        _ln_stats(w, c.ARF[:, 1024:2048], 1024)
        ins = None
        if w.eng == "V":
            w.res_wait("x2f")
            w.e.tensor_scalar(c.ARF[:, 2048:3072], c.ARF[:, 1024:2048],
                              c.stat[:, 2:3], c.stat[:, 1:2], ALU.subtract, ALU.mult)
            w.e.drain()
            w.e.tensor_tensor(c.ARF[:, 2048:3072], c.ARF[:, 2048:3072], c.nw_b[:], ALU.mult)
            w.e.drain()
            w.e.tensor_tensor(c.ARF[:, 2048:3072], c.ARF[:, 2048:3072], c.nb_b[:], ALU.add)
            w.e.drain()
            w.res_wait("x2b")
            w.e.tensor_copy(c.AR3[:, 1024:2048], c.ARF[:, 2048:3072])
            ins = w.e.drain()
        x2_v = w.inc("ev", ins)
        w.res_set("tf2", "ev")
        if w.eng == "Y":
            w.wait("ev", x2_v)
        w.dma("st_2c", c.acc_hbm[qt * 128:(qt + 1) * 128, :], c.ARF[:, 2048:3072])
        x2t_ev = _transpose_grp(w, c.AR3, 1024, c.AR3, 2048, 8, ("ev", x2_v),
                                dst_ready_key="x2t")
        w.res_set("x2b", "mm")
        if w.eng == "Y":
            w.wait("ev", x2t_ev)
        w.dma("st_2c",
              c.x2T_hbm[:, :, qt * 128:(qt + 1) * 128].rearrange("c p t -> p c t"),
              c.AR3[:, 2048:3072].rearrange("p (c t) -> p c t", c=8))
        w.res_set("x2t", "st_2c")
        w.res_set("x2f", "st_2c")

        ins = None
        if w.eng == "T":
            w.wait("ev", x2t_ev)
            w.res_wait("par")
            for ci in range(NC):
                ins = w.e.matmul(c.pa[0][:, 0:8],
                                 c.AR3[:, 2048 + ci * 128:2048 + (ci + 1) * 128],
                                 c.centbf[:, ci * 8:(ci + 1) * 8],
                                 start=(ci == 0), stop=(ci == NC - 1))
        lg_v = w.inc("mm", ins)
        ins = None
        if w.eng == "V":
            w.wait("mm", lg_v)
            w.e.tensor_copy(c.stat[:, 8:16], c.pa[0][:, 0:8])
            w.e.drain()
            w.e.reduce_max(c.stat[:, 16:17], c.stat[:, 8:16], axis=AX.X)
            w.e.drain()
            w.e.tensor_scalar(c.stat[:, 17:18], c.stat[:, 16:17], -1.0, None, ALU.mult)
            ins = w.e.drain()
        gv = w.inc("ev", ins)
        w.res_set("par", "ev")
        ins = None
        if w.eng == "S":
            w.wait("ev", gv)
            ins = w.e.activation(c.stat[:, 24:32], c.stat[:, 8:16], AF.Exp,
                                 bias=c.stat[:, 17:18], accum_out=c.stat[:, 32:33])
        sv = w.inc("sc", ins)
        ins = None
        if w.eng == "V":
            w.wait("sc", sv)
            w.e.reciprocal(c.stat[:, 33:34], c.stat[:, 32:33])
            w.e.drain()
            w.e.tensor_scalar(c.stat[:, 24:32], c.stat[:, 24:32], c.stat[:, 33:34],
                              None, ALU.mult)
            w.e.drain()
            w.e.tensor_tensor(c.stat[:, 40:48], c.stat[:, 24:32], c.rbias_b[:], ALU.add)
            w.e.drain()
            w.e.reduce_max(c.stat[:, 48:49], c.stat[:, 40:48], axis=AX.X)
            w.e.drain()
            w.e.tensor_scalar(c.stat[:, 50:58], c.stat[:, 40:48], c.stat[:, 48:49],
                              None, ALU.is_ge)
            w.e.drain()
            w.e.scalar_tensor_tensor(c.stat[:, 50:58], c.stat[:, 50:58], -1e30,
                                     c.stat[:, 40:48], ALU.mult, ALU.add)
            w.e.drain()
            w.e.reduce_max(c.stat[:, 48:49], c.stat[:, 50:58], axis=AX.X)
            w.e.drain()
            w.e.tensor_scalar(c.stat[:, 50:58], c.stat[:, 40:48], c.stat[:, 48:49],
                              None, ALU.is_ge)
            w.e.drain()
            w.e.tensor_tensor(c.stat[:, 50:58], c.stat[:, 50:58], c.stat[:, 24:32],
                              ALU.mult)
            w.e.drain()
            w.e.reduce_sum(c.stat[:, 48:49], c.stat[:, 50:58], axis=AX.X)
            w.e.drain()
            w.e.reciprocal(c.stat[:, 49:50], c.stat[:, 48:49])
            w.e.drain()
            ins = w.e.tensor_scalar(c.gates[:, qt * 8:(qt + 1) * 8], c.stat[:, 50:58],
                                    c.stat[:, 49:50], None, ALU.mult)
        w.inc("ev", ins)
    w.res_set("outw_done", "mm")




def _stageC(w):
    c = w.c

    def load_exp(e):
        sl = (e + 1) % 2
        if w.eng == "P" and e == 1:
            w.res_wait("outw_done")
        if w.eng in ("P", "Y") and e >= 2:
            w.res_wait(f"wslot{sl}")
        if e == 0:
            w1s, w2s = c.sw1, c.sw2
        else:
            w1s, w2s = c.rw1[e - 1], c.rw2[e - 1]
        stream = f"wl_w{sl}"
        for ci in range(NC):
            w.dma(stream, c.AR0[:, sl * 16384 + ci * 2048:sl * 16384 + (ci + 1) * 2048],
                  w1s[ci * 128:(ci + 1) * 128, :])
        for fi in range(NFF):
            w.dma(stream, c.AR1[:, sl * 16384 + fi * 1024:sl * 16384 + (fi + 1) * 1024],
                  w2s[fi * 128:(fi + 1) * 128, :])
        return (stream, w.cnt[stream])

    def load_bias(e):
        if w.eng == "P":
            w.res_wait("b2row")
        b2 = w.dma("wl_b2", c.row_c[:],
                   c.sb2_row if e == 0 else c.rb2_rows[e - 1:e, :])
        if w.eng == "Y":
            w.res_wait("b1c")
        b1 = w.dma("ld_b1c", c.b1c[:], c.sb1c if e == 0 else c.rb1c[e - 1])
        return b2, b1

    wcur = load_exp(0)
    for e in range(NEXP):
        sl = (e + 1) % 2
        bias_l = load_bias(e)
        wnxt = load_exp(e + 1) if e + 1 < NEXP else None
        b1 = c.b1c
        b2row = c.row_c
        for b in range(NBK):
            bs = b * 512
            xoff = 8192 + (b % 2) * 4096
            if w.eng == "Y":
                w.res_wait(f"x2tb{b % 2}")
                if e == 0:
                    w.wait("st_2c", 32 * (4 * b + 4))
            xst = f"ld_x2t{b % 2}"
            xb_l = w.dma(xst, c.AR2[:, xoff:xoff + 4096].rearrange("p (c t) -> p c t", c=8),
                         c.x2T_hbm[:, :, bs:bs + 512].rearrange("c p t -> p c t"))

            for fi in range(NFF):
                bk = fi % 4
                pbk = c.pb[bk]
                ins = None
                if w.eng == "T":
                    if fi == 0:
                        w.wait(wcur[0], wcur[1])
                        w.wait("wl_b2", bias_l[0])
                        w.wait(xst, xb_l)
                        w.res_wait("ht")
                    w.res_wait(f"pb{bk}")
                    for ci in range(NC):
                        ins = w.e.matmul(
                            pbk[:],
                            c.AR0[:, sl * 16384 + ci * 2048 + fi * 128:sl * 16384 + ci * 2048 + fi * 128 + 128],
                            c.AR2[:, xoff + ci * 512:xoff + (ci + 1) * 512],
                            start=(ci == 0), stop=(ci == NC - 1))
                mmv = w.inc("mm", ins)
                ins = None
                if w.eng == "S":
                    w.wait("mm", mmv)
                    if fi == 0:
                        w.wait("ld_b1c", bias_l[1])
                    ins = w.e.activation(c.AR2[:, fi * 512:(fi + 1) * 512], pbk[:],
                                         AF.Gelu, bias=b1[:, fi:fi + 1])
                w.inc("sc", ins)
                w.res_set(f"pb{bk}", "sc")
            h_v = w.cnt["sc"]
            for tt in range(4):
                for nb2 in range(2):
                    bk = (tt * 2 + nb2) % 4
                    pbk = c.pb[bk]
                    ins = None
                    if w.eng == "T":
                        if tt == 0 and nb2 == 0:
                            w.wait("sc", h_v)
                        w.res_wait(f"pb{bk}")
                        for fi in range(NFF):
                            w.e.matmul(
                                pbk[:],
                                c.AR2[:, fi * 512 + tt * 128:fi * 512 + tt * 128 + 128],
                                c.AR1[:, sl * 16384 + fi * 1024 + nb2 * 512:sl * 16384 + fi * 1024 + (nb2 + 1) * 512],
                                start=(fi == 0), stop=False)
                        ins = w.e.matmul(pbk[:], c.onesbf[:, 0:128],
                                         b2row[:, nb2 * 512:(nb2 + 1) * 512],
                                         start=False, stop=True)
                    mmv = w.inc("mm", ins)
                    ins = None
                    if w.eng == "V":
                        w.wait("mm", mmv)
                        if e == 0 and b == 0 and tt == 0 and nb2 == 0:
                            w.wait("ld_2c", w.cnt["ld_2c"])
                        if nb2 == 0 and tt % 2 == 0:
                            w.res_wait("ybuf2" if tt == 0 else "ybuf")
                        dst = c.ARF[:, (tt % 2) * 1024 + nb2 * 512:(tt % 2) * 1024 + (nb2 + 1) * 512]
                        if e == 0:
                            ins = w.e.tensor_copy(dst, pbk[:])
                        else:
                            gcol = c.gates[:, (b * 4 + tt) * 8 + (e - 1):(b * 4 + tt) * 8 + e]
                            ins = w.e.tensor_scalar(dst, pbk[:], gcol, None, ALU.mult)
                    yev = w.inc("ev", ins)
                    w.res_set(f"pb{bk}", "ev")
                if tt == 1:
                    seq = (e * NBK + b) * 2
                    if w.eng == "P":
                        w.wait("ev", yev)
                        w.wait("ga", 16 * seq)
                    w.dma("ga",
                          c.acc_hbm[bs:bs + 256, :].rearrange("(a p) n -> p a n", p=128),
                          c.ARF[:, 0:2048].rearrange("p (a n) -> p a n", a=2),
                          accum=ALU.add)
                    w.res_set("ybuf", "ga")
            y_v = w.cnt["ev"]
            seq = (e * NBK + b) * 2 + 1
            if w.eng == "P":
                w.wait("ev", y_v)
                w.wait("ga", 16 * seq)
            w.dma("ga",
                  c.acc_hbm[bs + 256:bs + 512, :].rearrange("(a p) n -> p a n", p=128),
                  c.ARF[:, 0:2048].rearrange("p (a n) -> p a n", a=2), accum=ALU.add)
            w.res_set("ybuf2", "ga")
            w.res_set(f"x2tb{b % 2}", "mm")
            w.res_set("ht", "mm")
        w.res_set(f"wslot{sl}", "mm")
        w.res_set("b2row", "mm")
        w.res_set("b1c", "sc")
        if wnxt is not None:
            wcur = wnxt




def _stageD(w):
    c = w.c
    ylv = w.dma("ld_pro", c.row_f[:], c.n2w_row)
    if w.eng == "T":
        w.wait("ld_pro", ylv)
    _bcast_row(w, c.row_f, c.nw_b, E, c.pb[0], "pb0")
    w.res_set("rowf", "mm")
    if w.eng == "Y":
        w.res_wait("rowf")
    nb_l = w.dma("ld_pro", c.row_f[:], c.n2b_row)
    if w.eng == "T":
        w.wait("ld_pro", nb_l)
    _bcast_row(w, c.row_f, c.nb_b, E, c.pb[1], "pb1")
    bc_v = w.cnt["ev"]

    for qt in range(NQT):
        if w.eng == "Y":
            w.res_wait("acct")
        ld1 = w.dma("ld_acc", c.ARF[:, 0:1024], c.acc_hbm[qt * 128:(qt + 1) * 128, :])
        if w.eng == "Y":
            w.res_wait("x1t3")
        ld2 = w.dma("ld_acc", c.ARF[:, 1024:2048], c.x1_hbm[qt * 128:(qt + 1) * 128, :])
        if w.eng == "V":
            w.wait("ld_acc", ld2)
            w.wait("ev", bc_v)
            w.res_wait("acct_w")
        _ln_stats(w, c.ARF[:, 0:1024], 1024)
        ins = None
        if w.eng == "V":
            w.res_wait("tf3")
            w.e.tensor_scalar(c.ARF[:, 2048:3072], c.ARF[:, 0:1024],
                              c.stat[:, 2:3], c.stat[:, 1:2], ALU.subtract, ALU.mult)
            w.e.drain()
            w.e.tensor_tensor(c.ARF[:, 2048:3072], c.ARF[:, 2048:3072], c.nw_b[:], ALU.mult)
            w.e.drain()
            w.e.tensor_tensor(c.ARF[:, 2048:3072], c.ARF[:, 2048:3072], c.nb_b[:], ALU.add)
            w.e.drain()
            w.e.tensor_tensor(c.ARF[:, 2048:3072], c.ARF[:, 2048:3072],
                              c.ARF[:, 1024:2048], ALU.add)
            ins = w.e.drain()
        w.inc("ev", ins)
        w.res_set("acct", "ev")
        w.res_set("x1t3", "ev")
        _ln_stats(w, c.ARF[:, 2048:3072], 1024)
        ins = None
        if w.eng == "V":
            ins = w.e.tensor_scalar(c.ARF[:, 0:1024], c.ARF[:, 2048:3072],
                                    c.stat[:, 2:3], c.stat[:, 1:2],
                                    ALU.subtract, ALU.mult)
        ov = w.inc("ev", ins)
        w.res_set("tf3", "ev")
        w.res_set("acct_w", "st_out", w.cnt["st_out"] + 16)
        if w.eng == "Y":
            w.wait("ev", ov)
        w.dma("st_out", c.out[qt * 128:(qt + 1) * 128, :], c.ARF[:, 0:1024])
        w.res_set("acct", "st_out")
    if w.eng == "Y":
        w.wait("st_out", w.cnt["st_out"])




def _hmaskc():
    m = np.zeros((128, 128), dtype=np.float32)
    for ci in range(8):
        m[0:64, ci * 16 + 2 * ci] = 1.0
        m[64:128, ci * 16 + 2 * ci + 1] = 1.0
    return m


def _stage_inputs(inputs):
    g = lambda k: np.asarray(inputs[k], dtype=np.float32)
    x = g("L0_attn_x")
    qkv_b = g("L0_ffn_qkv_b")
    shared = {
        "wqkv1": np.ascontiguousarray(g("L0_attn_W_qkv")),
        "wqkv2": np.ascontiguousarray(g("L0_ffn_qkv_w")),
        "wout": np.ascontiguousarray(g("L0_ffn_out_w")),
        "sw1": np.ascontiguousarray(g("L0_ffn_sw1")[0]),
        "sw2": np.ascontiguousarray(g("L0_ffn_sw2")[0]),
        "rw1": np.ascontiguousarray(g("L0_ffn_rw1")),
        "rw2": np.ascontiguousarray(g("L0_ffn_rw2")),
        "qkcols": np.ascontiguousarray(
            np.concatenate([qkv_b[0:E].reshape(NC, 128).T,
                            qkv_b[E:2 * E].reshape(NC, 128).T], axis=1)),
        "vb_row": np.ascontiguousarray(qkv_b[2 * E:3 * E].reshape(1, E)),
        "outb_row": np.ascontiguousarray(g("L0_ffn_out_b").reshape(1, E)),
        "n1w_row": np.ascontiguousarray(g("L0_ffn_norm1_w").reshape(1, E)),
        "n1b_row": np.ascontiguousarray(g("L0_ffn_norm1_b").reshape(1, E)),
        "n2w_row": np.ascontiguousarray(g("L0_ffn_norm2_w").reshape(1, E)),
        "n2b_row": np.ascontiguousarray(g("L0_ffn_norm2_b").reshape(1, E)),
        "sb1c": np.ascontiguousarray(g("L0_ffn_sb1")[0].reshape(NFF, 128).T),
        "sb2_row": np.ascontiguousarray(g("L0_ffn_sb2")[0].reshape(1, E)),
        "rb1c": np.ascontiguousarray(g("L0_ffn_rb1").reshape(8, NFF, 128).transpose(0, 2, 1)),
        "rb2_rows": np.ascontiguousarray(g("L0_ffn_rb2")),
        "centT": np.ascontiguousarray(g("L0_ffn_centroid").T.reshape(NC, 128, 8)),
        "rbias_row": np.ascontiguousarray(g("L0_ffn_route_bias").reshape(1, 8)),
        "ident": np.eye(128, dtype=np.float32),
        "cmask": np.triu(np.full((128, 128), -1e9, dtype=np.float32), k=1),
        "cmaskT": np.tril(np.full((128, 128), -1e9, dtype=np.float32), k=-1),
        "hmaskc": _hmaskc(),
    }
    maps = []
    for b in range(8):
        m = dict(shared)
        m["x0"] = np.ascontiguousarray(x[b])
        m["x0T"] = np.ascontiguousarray(x[b].T)
        maps.append(m)
    return maps


def _run(inputs, taps=(), nstages=8, apart=9):
    global _NSTAGES, _APART
    _NSTAGES = nstages
    _APART = apart
    nc, _ = _build(taps=taps)
    _NSTAGES = 8
    maps = _stage_inputs(inputs)
    return run_bass_kernel_spmd(nc, maps, core_ids=list(range(8)))


def kernel(**inputs):
    res = _run(inputs)
    return np.stack([r["out"] for r in res.results], axis=0)


# version 17
# speedup vs baseline: 1.0286x; 1.0286x over previous
import numpy as np
import concourse.bass as bass
import concourse.mybir as mybir
from concourse.bass_utils import run_bass_kernel_spmd
from contextlib import ExitStack

F32 = mybir.dt.float32
BF16 = mybir.dt.bfloat16
AF = mybir.ActivationFunctionType
ALU = mybir.AluOpType
AX = mybir.AxisListType

S = 4096
E = 1024
NC = 8
NQT = 32
NBK = 8
FF = 2048
NFF = 16
NEXP = 9
NPAIR = 8

_P_STREAMS = ("wl_pro", "wl_wkv", "wl_xb", "wl_wq", "wl_qkv2", "wl_out",
              "wl_w0", "wl_w1", "wl_b2", "ga")
_Y_STREAMS = ("ld_pro", "ld_x0", "ld_x1t0", "ld_x1t1", "ld_pair0", "ld_pair1",
              "ld_2c", "ld_x2t0", "ld_x2t1", "ld_b1c", "ld_acc",
              "ld_w20", "ld_w21",
              "st_x1", "st_x1t", "st_qk", "st_v2", "st_ot", "st_2c", "st_out")
_DMA_ENG = {}
for _s in _P_STREAMS:
    _DMA_ENG[_s] = "P"
for _s in _Y_STREAMS:
    _DMA_ENG[_s] = "Y"


class _Ctx:
    pass


class _Walk:
    def __init__(self, c, eng, e):
        self.c = c
        self.eng = eng
        self.e = e
        self.cnt = {k: 0 for k in c.semnames}
        self.res = {}
        self.tag = {}

    def inc(self, sem, ins=None, n=1):
        self.cnt[sem] += n
        if ins is not None:
            ins.then_inc(self.c.sem[sem], n)
        return self.cnt[sem]

    def wait(self, sem, val):
        if val > 0:
            self.e.wait_ge(self.c.sem[sem], val)

    def dma(self, sem, out, in_, accum=None):
        ins = None
        if self.eng == _DMA_ENG[sem]:
            if accum is None:
                ins = self.e.dma_start(out=out, in_=in_)
            else:
                ins = self.e.dma_start(out=out, in_=in_, accum_op=accum)
        return self.inc(sem, ins, 16)

    def res_wait(self, key):
        sv = self.res.get(key)
        if sv is not None:
            self.wait(sv[0], sv[1])

    def res_set(self, key, sem, val=None):
        self.res[key] = (sem, self.cnt[sem] if val is None else val)

    def bar(self):
        if self.eng == "Y":
            for s in _Y_STREAMS:
                self.wait(s, self.cnt[s])
        if self.eng == "P":
            for s in _P_STREAMS:
                self.wait(s, self.cnt[s])
        self.cnt["bar"] += 5
        self.e.drain()
        self.e.sem_inc(self.c.sem["bar"], 1)
        self.e.wait_ge(self.c.sem["bar"], self.cnt["bar"])


def _build(taps=()):
    nc = bass.Bass("TRN2", target_bir_lowering=False)
    c = _Ctx()
    c.nc = nc
    dt = lambda n, sh, d, k: nc.dram_tensor(n, sh, d, kind=k).ap()
    I, O = "ExternalInput", "ExternalOutput"
    TI = lambda n: O if n in taps else "Internal"

    c.x0 = dt("x0", [S, E], F32, I)
    c.x0T = dt("x0T", [E, S], F32, I)
    c.wqkv1 = dt("wqkv1", [E, 3 * E], F32, I)
    c.wqkv2 = dt("wqkv2", [E, 3 * E], F32, I)
    c.wout = dt("wout", [E, E], F32, I)
    c.sw1 = dt("sw1", [E, FF], F32, I)
    c.sw2 = dt("sw2", [FF, E], F32, I)
    c.rw1 = dt("rw1", [8, E, FF], F32, I)
    c.rw2 = dt("rw2", [8, FF, E], F32, I)
    c.qkcols = dt("qkcols", [128, 16], F32, I)
    c.vb_row = dt("vb_row", [1, E], F32, I)
    c.outb_row = dt("outb_row", [1, E], F32, I)
    c.n1w_row = dt("n1w_row", [1, E], F32, I)
    c.n1b_row = dt("n1b_row", [1, E], F32, I)
    c.n2w_row = dt("n2w_row", [1, E], F32, I)
    c.n2b_row = dt("n2b_row", [1, E], F32, I)
    c.sb1c = dt("sb1c", [128, NFF], F32, I)
    c.sb2_row = dt("sb2_row", [1, E], F32, I)
    c.rb1c = dt("rb1c", [8, 128, NFF], F32, I)
    c.rb2_rows = dt("rb2_rows", [8, E], F32, I)
    c.centT = dt("centT", [NC, 128, 8], F32, I)
    c.rbias_row = dt("rbias_row", [1, 8], F32, I)
    c.ident = dt("ident", [128, 128], F32, I)
    c.cmask = dt("cmask", [128, 128], F32, I)
    c.cmaskT = dt("cmaskT", [128, 128], F32, I)
    c.hmaskc = dt("hmaskc", [128, 128], F32, I)
    c.mrow_hbm = dt("mrow_hbm", [16, 1], F32, "Internal")

    c.x1_hbm = dt("x1_hbm", [S, E], F32, TI("x1_hbm"))
    c.x1T_hbm = dt("x1T_hbm", [NC, 128, S], BF16, TI("x1T_hbm"))
    c.q2T_hbm = dt("q2T_hbm", [NPAIR, 128, S], BF16, TI("q2T_hbm"))
    c.k2T_hbm = dt("k2T_hbm", [NPAIR, 128, S], BF16, TI("k2T_hbm"))
    c.v2_hbm = dt("v2_hbm", [NPAIR, S, 128], BF16, TI("v2_hbm"))
    c.oT_hbm = dt("oT_hbm", [NPAIR, 128, S], BF16, TI("oT_hbm"))
    c.acc_hbm = dt("acc_hbm", [S, E], F32, TI("acc_hbm"))
    c.x2T_hbm = dt("x2T_hbm", [NC, 128, S], BF16, TI("x2T_hbm"))
    c.out = dt("out", [S, E], F32, O)

    with ExitStack() as st:
        en = st.enter_context
        sb = lambda n, sh, d: en(nc.sbuf_tensor(n, sh, d))
        c.identbf = sb("identbf", [128, 128], BF16)
        c.cmask_s = sb("cmask_s", [128, 128], F32)
        c.cmaskT_s = sb("cmaskT_s", [128, 128], F32)
        c.hmask_s = sb("hmask_s", [128, 128], BF16)
        c.mbias = sb("mbias", [128, 16], F32)
        c.mrow_s = sb("mrow_s", [1, 16], F32)
        c.onesbf = sb("onesbf", [1, 128], BF16)
        c.onescolb = sb("onescolb", [128, 1], BF16)
        c.onesf = sb("onesf", [1, 128], F32)
        c.ones2 = sb("ones2", [128, 64], F32)
        c.epsT = sb("epsT", [128, 1], F32)
        c.qk_s = sb("qk_s", [128, 16], F32)
        c.qb8 = sb("qb8", [128, 8], F32)
        c.rbias_b = sb("rbias_b", [128, 8], F32)
        c.nw_b = sb("nw_b", [128, E], F32)
        c.nb_b = sb("nb_b", [128, E], F32)
        c.gates = sb("gates", [128, NQT * 8], F32)
        c.stat = sb("stat", [128, 60], F32)
        c.st6 = sb("st6", [128, 12], F32)
        c.row_a = sb("row_a", [1, E], BF16)
        c.row_c = sb("row_c", [1, E], BF16)
        c.row_c2 = sb("row_c2", [1, E], BF16)
        c.b1c2 = sb("b1c2", [128, NFF], F32)
        c.row_f = sb("row_f", [1, E], F32)
        c.row_r = sb("row_r", [1, 8], F32)
        c.b1c = sb("b1c", [128, NFF], F32)
        c.centbf = sb("centbf", [128, 64], BF16)
        c.AR0 = sb("AR0", [128, 32768], BF16)
        c.AR1 = sb("AR1", [128, 32768], BF16)
        c.AR2 = sb("AR2", [128, 16384], BF16)
        c.AR3 = sb("AR3", [128, 6144], BF16)
        c.ARF = sb("ARF", [128, 3072], F32)
        c.pb01 = en(nc.psum_tensor("pb01", [128, 1024], F32))
        c.pb23 = en(nc.psum_tensor("pb23", [128, 1024], F32))
        c.pb = [c.pb01[:, 0:512], c.pb01[:, 512:1024],
                c.pb23[:, 0:512], c.pb23[:, 512:1024]]
        c.pt = [en(nc.psum_tensor(f"pt{i}", [128, 1024], BF16)) for i in range(1)]
        c.pa = [en(nc.psum_tensor(f"pa{i}", [128, 512], F32)) for i in range(2)]
        c.pr = en(nc.psum_tensor("pr", [128, 512], F32))
        c.semnames = ("bar", "mm", "ev", "sc") + _P_STREAMS + _Y_STREAMS
        c.sem = {n: en(nc.semaphore(n)) for n in c.semnames}

        blk = en(nc.Block())

        def run(eng):
            def f(e):
                _emit(_Walk(c, eng, e))
            return f

        blk.tensor(run("T"))
        blk.vector(run("V"))
        blk.scalar(run("S"))
        blk.gpsimd(run("P"))
        blk.sync(run("Y"))
    return nc, c




def _ln_stats(w, src, n_free):
    c = w.c
    nchunk = (n_free + 511) // 512
    ins = None
    if w.eng == "V":
        for i in range(nchunk):
            a = i * 512
            w.e.bn_stats(c.st6[:, 6 * i:6 * i + 6], src[:, a:min(n_free, a + 512)])
        w.e.drain()
        w.e.bn_aggr(c.stat[:, 2:4], c.st6[:, 0:6 * nchunk])
        ins = w.e.drain()
    vstat = w.inc("ev", ins)
    ins = None
    if w.eng == "S":
        w.wait("ev", vstat)
        w.e.activation(c.stat[:, 4:5], c.stat[:, 3:4], AF.Ln, bias=c.epsT[:, 0:1])
        w.e.drain()
        ins = w.e.activation(c.stat[:, 1:2], c.stat[:, 4:5], AF.Exp, scale=-0.5)
    sdone = w.inc("sc", ins)
    if w.eng == "V":
        w.wait("sc", sdone)


def _bcast_row(w, row_ap, dst, n, pbank, bank_key):
    c = w.c
    for i in range((n + 511) // 512):
        a = i * 512
        b = min(n, a + 512)
        ins = None
        if w.eng == "T":
            w.res_wait(bank_key)
            ins = w.e.matmul(pbank[:, 0:b - a], c.onesf[:, 0:128], row_ap[:, a:b],
                             start=True, stop=True)
        v = w.inc("mm", ins)
        ins = None
        if w.eng == "V":
            w.wait("mm", v)
            ins = w.e.tensor_copy(dst[:, a:b], pbank[:, 0:b - a])
        w.inc("ev", ins)
        w.res_set(bank_key, "ev")


def _transpose_grp(w, buf, soff, dbuf, doff, ntiles, src_ready, dst_ready_key=None):
    c = w.c
    tiles = list(range(ntiles))
    ev_last = w.cnt["ev"]
    for gi in range(0, len(tiles), 4):
        g = tiles[gi:gi + 4]
        bank = (gi // 4) % len(c.pt)
        pt = c.pt[bank]
        ins = None
        if w.eng == "T":
            w.res_wait(f"pt{bank}")
            if gi == 0 and src_ready is not None:
                w.wait(src_ready[0], src_ready[1])
            for j, t in enumerate(g):
                ins = w.e.transpose(pt[:, j * 128:(j + 1) * 128],
                                    buf[:, soff + t * 128:soff + (t + 1) * 128],
                                    c.identbf[:])
        mmv = w.inc("mm", ins)
        ins = None
        if w.eng == "V":
            w.wait("mm", mmv)
            if gi == 0 and dst_ready_key is not None:
                w.res_wait(dst_ready_key)
            a = g[0] * 128
            b = (g[-1] + 1) * 128
            ins = w.e.tensor_copy(dbuf[:, doff + a:doff + b], pt[:, 0:b - a])
        ev_last = w.inc("ev", ins)
        w.res_set(f"pt{bank}", "ev")
    return ev_last




_NSTAGES = 8
_APART = 9


def _emit(w):
    stages = [_prologue, _stageA_kv, _stageA_attn, _stageB_qkv, _stageB_attn,
              _stageB_proj, _stageC, _stageD]
    for i, f in enumerate(stages[:_NSTAGES]):
        f(w)
        if i < len(stages) - 1 and stages[i + 1] is not _stageC:
            w.bar()


def _prologue(w):
    c = w.c
    w.dma("wl_pro", c.identbf[:], c.ident)
    w.dma("wl_pro", c.row_a[:], c.vb_row)
    w.dma("wl_pro", c.hmask_s[:], c.hmaskc)
    w.dma("wl_pro", c.centbf[:].rearrange("p (a f) -> p a f", a=8),
          c.centT.rearrange("c p f -> p c f"))
    wl = w.cnt["wl_pro"]
    w.dma("ld_pro", c.cmask_s[:], c.cmask)
    w.dma("ld_pro", c.cmaskT_s[:], c.cmaskT)
    w.dma("ld_pro", c.qk_s[:], c.qkcols)
    w.dma("ld_pro", c.row_r[:], c.rbias_row)
    w.dma("ld_pro", c.row_f[:], c.n1w_row)
    yl = w.cnt["ld_pro"]
    ins = None
    if w.eng == "V":
        w.wait("ld_pro", yl)
        w.wait("wl_pro", wl)
        w.e.memset(c.epsT[:], 1e-5)
        w.e.memset(c.onesbf[:], 1.0)
        w.e.memset(c.onescolb[:], 1.0)
        w.e.memset(c.onesf[:], 1.0)
        w.e.memset(c.ones2[:], 1.0)
        w.e.tensor_scalar(c.qb8[:], c.qk_s[:, 0:8], 0.125, None, ALU.mult)
        ins = w.e.drain()
    mv = w.inc("ev", ins)
    if w.eng == "T":
        w.wait("ev", mv)
    _bcast_row(w, c.row_r, c.rbias_b, 8, c.pb[0], "pb0")
    _bcast_row(w, c.row_f, c.nw_b, E, c.pb[1], "pb1")
    w.res_set("rowf", "mm")
    if w.eng == "Y":
        w.res_wait("rowf")
    nb_l = w.dma("ld_pro", c.row_f[:], c.n1b_row)
    if w.eng == "T":
        w.wait("ld_pro", nb_l)
    _bcast_row(w, c.row_f, c.nb_b, E, c.pb[2], "pb2")




def _attn_sweep1(w, qt, cfg, par):
    c = w.c
    span = (qt + 1) * 128
    nch = (span + 511) // 512
    heads = cfg["heads"]
    for ch in range(nch):
        cw = min(512, span - ch * 512)
        for h in heads:
            sb = h["sb"][par]
            bank = h["banks"][ch % 2]
            ins = None
            if w.eng == "T":
                if ch == 0 and cfg.get("qk_ready") is not None:
                    w.wait(cfg["qk_ready"][0], cfg["qk_ready"][1])
                w.res_wait(bank[1])
                nci = h["nci"]
                for ci in range(nci):
                    ins = w.e.matmul(bank[0][:, 0:cw], h["lhsT"](ci, qt),
                                     h["rhs"](ci, ch * 512, cw),
                                     start=(ci == 0), stop=(ci == nci - 1),
                                     **h["mmkw"])
            mmv = w.inc("mm", ins)
            ins = None
            if w.eng == "V":
                w.wait("mm", mmv)
                if ch == nch - 1:
                    w.e.tensor_tensor(bank[0][:, cw - 128:cw], bank[0][:, cw - 128:cw],
                                      c.cmask_s[:], ALU.add)
                    w.e.drain()
                ins = w.e.reduce_max(c.stat[:, sb + ch:sb + ch + 1],
                                     bank[0][:, 0:cw], axis=AX.X)
            w.inc("ev", ins)
            w.res_set(bank[1], "ev")
    ins = None
    if w.eng == "V":
        w.e.drain()
        for h in heads:
            sb = h["sb"][par]
            w.e.reduce_max(c.stat[:, sb + 8:sb + 9], c.stat[:, sb:sb + nch], axis=AX.X)
        w.e.drain()
        for h in heads:
            sb = h["sb"][par]
            w.e.tensor_scalar(c.stat[:, sb + 9:sb + 10], c.stat[:, sb + 8:sb + 9],
                              -1.0, None, ALU.mult)
        ins = w.e.drain()
    return w.inc("ev", ins)


def _attn_sweep2(w, qt, cfg, par, mneg_v):
    c = w.c
    span = (qt + 1) * 128
    nch = (span + 511) // 512
    heads = cfg["heads"]
    for ch in range(nch):
        cw = min(512, span - ch * 512)
        for h in heads:
            sb = h["sb"][par]
            poff = h["poff"](par)
            pbuf = h["pbuf"]
            bank = h["banks"][ch % 2]
            ins = None
            if w.eng == "T":
                if ch == 0 and cfg.get("qk_ready") is not None:
                    w.wait(cfg["qk_ready"][0], cfg["qk_ready"][1])
                w.res_wait(bank[1])
                nci = h["nci"]
                for ci in range(nci):
                    ins = w.e.matmul(bank[0][:, 0:cw], h["lhsT"](ci, qt),
                                     h["rhs"](ci, ch * 512, cw),
                                     start=(ci == 0), stop=(ci == nci - 1),
                                     **h["mmkw"])
            mmv = w.inc("mm", ins)
            if ch == nch - 1:
                ins = None
                if w.eng == "V":
                    w.wait("mm", mmv)
                    ins = w.e.tensor_tensor(bank[0][:, cw - 128:cw],
                                            bank[0][:, cw - 128:cw],
                                            c.cmask_s[:], ALU.add)
                mk = w.inc("ev", ins)
            else:
                mk = w.cnt["ev"]
            ins = None
            if w.eng == "S":
                w.wait("mm", mmv)
                w.wait("ev", max(mneg_v, mk))
                if ch == 0:
                    w.res_wait(h["pkey"] + str(par))
                ins = w.e.activation(pbuf[:, poff + ch * 512:poff + ch * 512 + cw],
                                     bank[0][:, 0:cw], AF.Exp,
                                     bias=c.stat[:, sb + 9:sb + 10],
                                     accum_out=c.stat[:, sb + 10 + ch:sb + 11 + ch])
            w.inc("sc", ins)
            w.res_set(bank[1], "sc")
    exp_v = w.cnt["sc"]
    ins = None
    if w.eng == "V":
        w.wait("sc", exp_v)
        for h in heads:
            sb = h["sb"][par]
            w.e.reduce_sum(c.stat[:, sb + 18:sb + 19], c.stat[:, sb + 10:sb + 10 + nch],
                           axis=AX.X)
        w.e.drain()
        for h in heads:
            sb = h["sb"][par]
            w.e.reciprocal(c.stat[:, sb + 19:sb + 20], c.stat[:, sb + 18:sb + 19])
        w.e.drain()
        for h in heads:
            sb = h["sb"][par]
            poff = h["poff"](par)
            w.e.tensor_scalar(h["pbuf"][:, poff:poff + span],
                              h["pbuf"][:, poff:poff + span],
                              c.stat[:, sb + 19:sb + 20], None, ALU.mult)
        ins = w.e.drain()
    return w.inc("ev", ins)


def _attn_transpose(w, qt, cfg, par, norm_v, evac_eng):
    c = w.c
    span = (qt + 1) * 128
    out = []
    for hi, h in enumerate(cfg["heads"]):
        poff = h["poff"](par)
        pbuf = h["pbuf"]
        eng = evac_eng[hi % len(evac_eng)]
        ntiles = span // 128
        ev_last = (("ev", w.cnt["ev"]) if eng == "V" else ("sc", w.cnt["sc"]))
        for gi in range(0, ntiles, 4):
            g = list(range(gi, min(gi + 4, ntiles)))
            banki = (gi // 4) % len(c.pt)
            pt = c.pt[banki]
            ins = None
            if w.eng == "T":
                w.res_wait(f"pt{banki}")
                if gi == 0:
                    w.wait("ev", norm_v)
                for t in g:
                    ins = w.e.transpose(pt[:, (t - gi) * 128:(t - gi + 1) * 128],
                                        pbuf[:, poff + t * 128:poff + (t + 1) * 128],
                                        c.identbf[:])
            mmv = w.inc("mm", ins)
            a, b = g[0] * 128, (g[-1] + 1) * 128
            ins = None
            if w.eng == eng:
                w.wait("mm", mmv)
                if eng == "V":
                    ins = w.e.tensor_copy(pbuf[:, poff + a:poff + b], pt[:, 0:b - a])
                else:
                    ins = w.e.activation(pbuf[:, poff + a:poff + b], pt[:, 0:b - a],
                                         AF.Copy)
            sem = "ev" if eng == "V" else "sc"
            ev_last = (sem, w.inc(sem, ins))
            w.res_set(f"pt{banki}", sem)
        out.append(ev_last)
    return out


def _stageA(w):
    _stageA_kv(w)
    w.bar()
    _stageA_attn(w)


def _stageA_kv(w):
    c = w.c
    KT, V = c.AR0, c.AR1

    for ci in range(NC):
        w.dma("wl_wkv", c.AR2[:, ci * 2048:(ci + 1) * 2048],
              c.wqkv1[ci * 128:(ci + 1) * 128, E:3 * E])
    wkv_l = w.cnt["wl_wkv"]

    for b in range(NBK):
        bs = b * 512
        if w.eng == "P":
            w.res_wait("xb")
        xb_l = w.dma("wl_xb", c.AR3[:, 0:4096].rearrange("p (c t) -> p c t", c=8),
                     c.x0T[:, bs:bs + 512].rearrange("(c p) t -> p c t", p=128))
        for nco in range(NC):
            pbk = c.pb[nco % 4]
            ins = None
            if w.eng == "T":
                w.wait("wl_wkv", wkv_l)
                w.wait("wl_xb", xb_l)
                w.res_wait(f"pb{nco % 4}")
                for ci in range(NC):
                    ins = w.e.matmul(pbk[:],
                                     c.AR2[:, ci * 2048 + nco * 128:ci * 2048 + nco * 128 + 128],
                                     c.AR3[:, ci * 512:(ci + 1) * 512],
                                     start=(ci == 0), stop=(ci == NC - 1))
            mmv = w.inc("mm", ins)
            ins = None
            if w.eng == "S":
                w.wait("mm", mmv)
                ins = w.e.activation(KT[:, nco * 4096 + bs:nco * 4096 + bs + 512],
                                     pbk[:], AF.Copy)
            w.inc("sc", ins)
            w.res_set(f"pb{nco % 4}", "sc")
        for tt in range(4):
            for nb2 in range(2):
                bk = (tt * 2 + nb2) % 4
                pbk = c.pb[bk]
                ins = None
                if w.eng == "T":
                    w.res_wait(f"pb{bk}")
                    for ci in range(NC):
                        ins = w.e.matmul(pbk[:],
                                         c.AR3[:, ci * 512 + tt * 128:ci * 512 + tt * 128 + 128],
                                         c.AR2[:, ci * 2048 + 1024 + nb2 * 512:ci * 2048 + 1024 + (nb2 + 1) * 512],
                                         start=(ci == 0), stop=(ci == NC - 1))
                mmv = w.inc("mm", ins)
                ins = None
                if w.eng == "V":
                    w.wait("mm", mmv)
                    ins = w.e.tensor_copy(
                        V[:, (b * 4 + tt) * 1024 + nb2 * 512:(b * 4 + tt) * 1024 + (nb2 + 1) * 512],
                        pbk[:])
                w.inc("ev", ins)
                w.res_set(f"pb{bk}", "ev")
        w.res_set("xb", "mm")


def _stageA_attn(w):
    c = w.c
    KT, V = c.AR0, c.AR1
    for ci in range(NC):
        w.dma("wl_wq", c.AR2[:, ci * 1024:(ci + 1) * 1024],
              c.wqkv1[ci * 128:(ci + 1) * 128, 0:E])
    wq_l = w.cnt["wl_wq"]

    for b in range(NBK):
        for nco in range(NC):
            sl = (b * NC + nco) % 2
            scr = c.AR3[:, sl * 512:(sl + 1) * 512]
            ins = None
            if w.eng == "V":
                w.res_wait(f"ksq{sl}")
                src = KT[:, nco * 4096 + b * 512:nco * 4096 + (b + 1) * 512]
                ins = w.e.tensor_tensor(scr, src, src, ALU.mult)
            kv = w.inc("ev", ins)
            ins = None
            if w.eng == "T":
                w.wait("ev", kv)
                if nco == 0:
                    w.res_wait("prq")
                ins = w.e.matmul(c.pr[0:1, 0:512], c.onescolb[:, 0:1], scr,
                                 start=(nco == 0), stop=(nco == NC - 1))
            w.inc("mm", ins)
            w.res_set(f"ksq{sl}", "mm")
        kmm = w.cnt["mm"]
        ins = None
        if w.eng == "V":
            w.wait("mm", kmm)
            ins = w.e.reduce_max(c.stat[0:1, 16 + b:17 + b], c.pr[0:1, 0:512],
                                 axis=AX.X)
        w.inc("ev", ins)
        w.res_set("prq", "ev")
    ins = None
    if w.eng == "V":
        w.e.drain()
        ins = w.e.reduce_max(c.stat[0:1, 24:25], c.stat[0:1, 16:24], axis=AX.X)
    gkv = w.inc("ev", ins)
    ins = None
    if w.eng == "T":
        w.wait("ev", gkv)
        w.res_wait("pb0")
        ins = w.e.matmul(c.pb[0][0:128, 0:1], c.onesf[0:1, 0:128],
                         c.stat[0:1, 24:25], start=True, stop=True)
    bkm = w.inc("mm", ins)
    ins = None
    if w.eng == "V":
        w.wait("mm", bkm)
        ins = w.e.tensor_copy(c.stat[:, 58:59], c.pb[0][:, 0:1])
    w.inc("ev", ins)
    w.res_set("pb0", "ev")

    for qb in range(NBK):
        bs = qb * 512
        if w.eng == "P":
            w.res_wait("xb")
        xb_l = w.dma("wl_xb", c.AR2[:, 12288:16384].rearrange("p (c t) -> p c t", c=8),
                     c.x0T[:, bs:bs + 512].rearrange("(c p) t -> p c t", p=128))
        for nco in range(NC):
            pbk = c.pb[nco % 4]
            ins = None
            if w.eng == "T":
                w.wait("wl_wq", wq_l)
                w.wait("wl_xb", xb_l)
                w.res_wait(f"pb{nco % 4}")
                if nco == 0:
                    w.res_wait("qtbuf")
                for ci in range(NC):
                    ins = w.e.matmul(pbk[:],
                                     c.AR2[:, ci * 1024 + nco * 128:ci * 1024 + nco * 128 + 128],
                                     c.AR2[:, 12288 + ci * 512:12288 + (ci + 1) * 512],
                                     start=(ci == 0), stop=(ci == NC - 1))
            mmv = w.inc("mm", ins)
            ins = None
            if w.eng == "S":
                w.wait("mm", mmv)
                ins = w.e.activation(c.AR2[:, 8192 + nco * 512:8192 + (nco + 1) * 512],
                                     pbk[:], AF.Copy, scale=1.0 / 32.0)
            w.inc("sc", ins)
            w.res_set(f"pb{nco % 4}", "sc")
        qt_ready = ("sc", w.cnt["sc"])
        w.res_set("xb", "mm")

        for ql in range(4):
            qt = qb * 4 + ql
            if w.eng == "Y":
                w.res_wait("x0t")
            x0_l = w.dma("ld_x0", c.ARF[:, 0:1024], c.x0[qt * 128:(qt + 1) * 128, :])

            heads = [dict(
                lhsT=lambda ci, q, _ql=ql: c.AR2[:, 8192 + ci * 512 + _ql * 128:8192 + ci * 512 + _ql * 128 + 128],
                rhs=lambda ci, ks, kw: KT[:, ci * 4096 + ks:ci * 4096 + ks + kw],
                nci=NC, mmkw={}, banks=[(c.pb[0], "pb0"), (c.pb[1], "pb1")],
                sb={0: 8}, poff=lambda par: 0, pbuf=c.AR3, pkey="PA",
            )]
            cfg = {"heads": heads, "qk_ready": qt_ready}
            span = (qt + 1) * 128

            ins = None
            if w.eng == "V":
                w.res_wait("x1b")
                w.wait("sc", qt_ready[1])
                for ci in range(NC):
                    src = c.AR2[:, 8192 + ci * 512 + ql * 128:8192 + ci * 512 + ql * 128 + 128]
                    w.e.tensor_tensor(c.AR3[:, 4096 + ci * 128:4096 + (ci + 1) * 128],
                                      src, src, ALU.mult)
                ins = w.e.drain()
            sqv = w.inc("ev", ins)
            ins = None
            if w.eng == "T":
                w.wait("ev", sqv)
                w.res_wait("prq")
                for ci in range(NC):
                    ins = w.e.matmul(c.pr[0:128, 0:1],
                                     c.AR3[:, 4096 + ci * 128:4096 + (ci + 1) * 128],
                                     c.onescolb[:, 0:1],
                                     start=(ci == 0), stop=(ci == NC - 1))
            qmm = w.inc("mm", ins)
            w.res_set("x1b", "mm")
            ins = None
            if w.eng == "V":
                w.wait("mm", qmm)
                ins = w.e.tensor_tensor(c.stat[:, 50:51], c.pr[:, 0:1],
                                        c.stat[:, 58:59], ALU.mult)
            q2v = w.inc("ev", ins)
            w.res_set("prq", "ev")
            ins = None
            if w.eng == "S":
                w.wait("ev", q2v)
                w.e.activation(c.stat[:, 51:52], c.stat[:, 50:51], AF.Ln)
                w.e.drain()
                ins = w.e.activation(c.stat[:, 52:53], c.stat[:, 51:52],
                                     AF.Exp, scale=0.5)
            qsv = w.inc("sc", ins)
            ins = None
            if w.eng == "V":
                w.wait("sc", qsv)
                ins = w.e.tensor_scalar(c.stat[:, 17:18], c.stat[:, 52:53],
                                        -1.0, None, ALU.mult)
            mneg_v = w.inc("ev", ins)

            norm_v = _attn_sweep2(w, qt, cfg, 0, mneg_v)
            evs = _attn_transpose(w, qt, cfg, 0, norm_v, ["V"])
            ev = evs[0]

            ins = None
            if w.eng == "T":
                w.wait(ev[0], ev[1])
                w.res_wait("pa0")
                w.res_wait("pa1")
                last = span // 128 - 1
                for kt in range(span // 128):
                    w.e.matmul(c.pa[0][:], c.AR3[:, kt * 128:(kt + 1) * 128],
                               V[:, kt * 1024:kt * 1024 + 512],
                               start=(kt == 0), stop=(kt == last))
                    ins = w.e.matmul(c.pa[1][:], c.AR3[:, kt * 128:(kt + 1) * 128],
                                     V[:, kt * 1024 + 512:(kt + 1) * 1024],
                                     start=(kt == 0), stop=(kt == last))
            av_v = w.inc("mm", ins)
            w.res_set("PA0", "mm")

            ins = None
            if w.eng == "V":
                w.wait("mm", av_v)
                w.wait("ld_x0", x0_l)
                w.res_wait("tf")
                w.e.tensor_tensor(c.ARF[:, 1024:1536], c.pa[0][:], c.ARF[:, 0:512], ALU.add)
                w.e.tensor_tensor(c.ARF[:, 1536:2048], c.pa[1][:], c.ARF[:, 512:1024], ALU.add)
                ins = w.e.drain()
            w.inc("ev", ins)
            w.res_set("pa0", "ev")
            w.res_set("pa1", "ev")
            w.res_set("x0t", "ev")
            _ln_stats(w, c.ARF[:, 1024:2048], 1024)
            ins = None
            if w.eng == "V":
                w.res_wait("x1f")
                w.e.tensor_scalar(c.ARF[:, 2048:3072], c.ARF[:, 1024:2048],
                                  c.stat[:, 2:3], c.stat[:, 1:2], ALU.subtract, ALU.mult)
                w.e.drain()
                w.res_wait("x1b")
                w.e.tensor_copy(c.AR3[:, 4096:5120], c.ARF[:, 2048:3072])
                ins = w.e.drain()
            x1_v = w.inc("ev", ins)
            w.res_set("tf", "ev")
            if w.eng == "Y":
                w.wait("ev", x1_v)
            w.dma("st_x1", c.x1_hbm[qt * 128:(qt + 1) * 128, :], c.ARF[:, 2048:3072])
            w.res_set("x1f", "st_x1")
            x1t_ev = _transpose_grp(w, c.AR3, 4096, c.AR3, 5120, 8, ("ev", x1_v),
                                    dst_ready_key="x1t")
            w.res_set("x1b", "mm")
            if w.eng == "Y":
                w.wait("ev", x1t_ev)
            w.dma("st_x1t",
                  c.x1T_hbm[:, :, qt * 128:(qt + 1) * 128].rearrange("c p t -> p c t"),
                  c.AR3[:, 5120:6144].rearrange("p (c t) -> p c t", c=8))
            w.res_set("x1t", "st_x1t")
        w.res_set("qtbuf", "mm")




def _stageB_qkv(w):
    c = w.c
    for ci in range(NC):
        w.dma("wl_qkv2", c.AR0[:, ci * 3072:(ci + 1) * 3072],
              c.wqkv2[ci * 128:(ci + 1) * 128, :])
    wl_v = w.cnt["wl_qkv2"]

    for b in range(NBK):
        bs = b * 512
        xoff = (b % 2) * 4096
        if w.eng == "Y":
            w.res_wait(f"x1tb{b % 2}")
        xst = f"ld_x1t{b % 2}"
        xb_l = w.dma(xst, c.AR1[:, xoff:xoff + 4096].rearrange("p (c t) -> p c t", c=8),
                     c.x1T_hbm[:, :, bs:bs + 512].rearrange("c p t -> p c t"))

        for nco in range(16):
            bk = nco % 4
            pbk = c.pb[bk]
            ins = None
            if w.eng == "T":
                w.wait("wl_qkv2", wl_v)
                w.wait(xst, xb_l)
                w.res_wait(f"pb{bk}")
                for ci in range(NC):
                    ins = w.e.matmul(pbk[:],
                                     c.AR0[:, ci * 3072 + nco * 128:ci * 3072 + nco * 128 + 128],
                                     c.AR1[:, xoff + ci * 512:xoff + (ci + 1) * 512],
                                     start=(ci == 0), stop=(ci == NC - 1))
            mmv = w.inc("mm", ins)
            so = bk * 512
            ins = None
            if w.eng == "V":
                w.wait("mm", mmv)
                w.res_wait(f"qks{bk}")
                if nco < 8:
                    ins = w.e.tensor_scalar(c.AR3[:, so:so + 512], pbk[:], 0.125,
                                            c.qb8[:, nco:nco + 1], ALU.mult, ALU.add)
                else:
                    ins = w.e.tensor_scalar(c.AR3[:, so:so + 512], pbk[:],
                                            c.qk_s[:, nco:nco + 1], None, ALU.add)
            scv = w.inc("ev", ins)
            w.res_set(f"pb{bk}", "ev")
            if w.eng == "Y":
                w.wait("ev", scv)
            tgt = c.q2T_hbm if nco < 8 else c.k2T_hbm
            w.dma("st_qk", tgt[nco % 8, :, bs:bs + 512], c.AR3[:, so:so + 512])
            w.res_set(f"qks{bk}", "st_qk")

            sqo = 3072 + (nco % 2) * 512
            ci = nco % 8
            ins = None
            if w.eng == "V":
                w.res_wait(f"sq{nco % 2}")
                w.e.tensor_tensor(c.AR3[:, sqo:sqo + 512], c.AR3[:, so:so + 512],
                                  c.AR3[:, so:so + 512], ALU.mult)
                ins = w.e.drain()
            sqv = w.inc("ev", ins)
            ins = None
            if w.eng == "T":
                w.wait("ev", sqv)
                if nco in (0, 8):
                    w.res_wait("pa0")
                ins = w.e.matmul(c.pa[0][0:16, 0:512],
                                 c.hmask_s[:, ci * 16:(ci + 1) * 16],
                                 c.AR3[:, sqo:sqo + 512],
                                 start=(nco in (0, 8)), stop=(nco in (7, 15)))
            sqm = w.inc("mm", ins)
            w.res_set(f"sq{nco % 2}", "mm")
            if nco in (7, 15):
                col = (30 + b) if nco == 7 else (38 + b)
                ins = None
                if w.eng == "V":
                    w.wait("mm", sqm)
                    ins = w.e.reduce_max(c.stat[0:16, col:col + 1],
                                         c.pa[0][0:16, 0:512], axis=AX.X)
                w.inc("ev", ins)
                w.res_set("pa0", "ev")
        for tt in range(4):
            for nb2 in range(2):
                bk = (tt * 2 + nb2) % 4
                pbk = c.pb[bk]
                ins = None
                if w.eng == "T":
                    w.res_wait(f"pb{bk}")
                    for ci in range(NC):
                        w.e.matmul(pbk[:],
                                   c.AR1[:, xoff + ci * 512 + tt * 128:xoff + ci * 512 + tt * 128 + 128],
                                   c.AR0[:, ci * 3072 + 2048 + nb2 * 512:ci * 3072 + 2048 + (nb2 + 1) * 512],
                                   start=(ci == 0), stop=False)
                    ins = w.e.matmul(pbk[:], c.onesbf[:, 0:128],
                                     c.row_a[:, nb2 * 512:(nb2 + 1) * 512],
                                     start=False, stop=True)
                mmv = w.inc("mm", ins)
                vo = 2048 + ((tt * 2 + nb2) % 2) * 512
                ins = None
                if w.eng == "V":
                    w.wait("mm", mmv)
                    w.res_wait(f"v2s{(tt * 2 + nb2) % 2}")
                    ins = w.e.tensor_copy(c.AR3[:, vo:vo + 512], pbk[:])
                evv = w.inc("ev", ins)
                w.res_set(f"pb{bk}", "ev")
                if w.eng == "Y":
                    w.wait("ev", evv)
                for j in range(4):
                    w.dma("st_v2", c.v2_hbm[nb2 * 4 + j, bs + tt * 128:bs + (tt + 1) * 128, :],
                          c.AR3[:, vo + j * 128:vo + (j + 1) * 128])
                w.res_set(f"v2s{(tt * 2 + nb2) % 2}", "st_v2")
        w.res_set(f"x1tb{b % 2}", "mm")

    ins = None
    if w.eng == "V":
        w.e.drain()
        w.e.reduce_max(c.stat[0:16, 46:47], c.stat[0:16, 30:38], axis=AX.X)
        w.e.reduce_max(c.stat[0:16, 47:48], c.stat[0:16, 38:46], axis=AX.X)
        w.e.drain()
        w.e.tensor_tensor(c.stat[0:16, 48:49], c.stat[0:16, 46:47],
                          c.stat[0:16, 47:48], ALU.mult)
        ins = w.e.drain()
    mv = w.inc("ev", ins)
    if w.eng == "Y":
        w.wait("ev", mv)
    st1 = w.dma("st_qk", c.mrow_hbm[:, :], c.stat[0:16, 48:49])
    if w.eng == "Y":
        w.wait("st_qk", st1)
    ld1 = w.dma("ld_pro", c.mrow_s[:], c.mrow_hbm.rearrange("p o -> o p"))
    ins = None
    if w.eng == "V":
        w.wait("ld_pro", ld1)
        pairs = c.mrow_s[:, 0:16].rearrange("o (a b) -> o a b", b=2)
        ins = w.e.tensor_tensor(c.row_r[:, 0:8].rearrange("o (a b) -> o a b", b=1),
                                pairs[:, :, 0:1], pairs[:, :, 1:2], ALU.max)
    pv = w.inc("ev", ins)
    ins = None
    if w.eng == "S":
        w.wait("ev", pv)
        ins = w.e.activation(c.row_r[0:1, 0:8], c.row_r[0:1, 0:8], AF.Sqrt)
    sv = w.inc("sc", ins)
    ins = None
    if w.eng == "V":
        w.wait("sc", sv)
        ins = w.e.tensor_scalar(c.row_r[0:1, 0:8], c.row_r[0:1, 0:8],
                                -1.0, None, ALU.mult)
    nv = w.inc("ev", ins)
    ins = None
    if w.eng == "T":
        w.wait("ev", nv)
        w.res_wait("pr")
        ins = w.e.matmul(c.pr[0:128, 0:8], c.onesf[0:1, 0:128],
                         c.row_r[0:1, 0:8], start=True, stop=True)
    bv = w.inc("mm", ins)
    ins = None
    if w.eng == "V":
        w.wait("mm", bv)
        ins = w.e.tensor_copy(c.mbias[:, 0:8], c.pr[:, 0:8])
    w.inc("ev", ins)
    w.res_set("pr", "ev")


def _stageB_attn(w):
    c = w.c
    PT = (8320, 9344)
    OTA, OTB = 10368, 10880
    ps = [c.pb01, c.pb23]
    avA = c.pa[0]
    avB = c.pa[1]
    lbc = c.pr

    ins = None
    if w.eng == "V":
        for s in range(2):
            w.e.memset(c.AR1[:, s * 4160:(s + 1) * 4160], 1.0)
        ins = w.e.drain()
    msv = w.inc("ev", ins)

    def load_pair(hp):
        s = hp % 2
        b0 = s * 8192
        vb = s * 4160
        stream = f"ld_pair{s}"
        if w.eng == "Y":
            if hp < 2:
                w.wait("ev", msv)
            w.res_wait(f"pair{s}")
        w.dma(stream, c.AR0[:, b0:b0 + 4096], c.q2T_hbm[hp, :, :])
        w.dma(stream, c.AR0[:, b0 + 4096:b0 + 8192], c.k2T_hbm[hp, :, :])
        vv = c.AR1[:, vb:vb + 4160].rearrange("p (a f) -> p a f", f=130)
        src = c.v2_hbm[hp, :, :].rearrange("(a p) f -> p a f", p=128)
        w.dma(stream, vv[:, :, 0:64], src[:, :, 0:64])
        return (stream, w.dma(stream, vv[:, :, 65:129], src[:, :, 64:128]))

    pend_epi = [None, None]
    pair_l = load_pair(0)
    for hp in range(NPAIR):
        s = hp % 2
        b0 = s * 8192
        vb = s * 4160
        vv = c.AR1[:, vb:vb + 4160].rearrange("p (a f) -> p a f", f=130)
        nxt = load_pair(hp + 1) if hp + 1 < NPAIR else None
        for qb in range(8):
            ktmax = 4 * qb + 3
            pend_av = None
            for kt in range(ktmax + 1):
                jd = kt - 4 * qb
                off = max(0, jd) * 128
                width = 512 - off
                sl = kt % 2
                psl = ps[sl]
                ins = None
                if w.eng == "T":
                    if qb == 0 and kt == 0:
                        w.wait(pair_l[0], pair_l[1])
                    w.res_wait(f"pscore{sl}")
                    w.e.matmul(psl[:, 0:width],
                               c.AR0[0:64, b0 + 4096 + kt * 128:b0 + 4096 + (kt + 1) * 128],
                               c.AR0[0:64, b0 + qb * 512 + off:b0 + (qb + 1) * 512],
                               start=True, stop=True, tile_position=(0, 0))
                    ins = w.e.matmul(psl[:, 512:512 + width],
                               c.AR0[64:128, b0 + 4096 + kt * 128:b0 + 4096 + (kt + 1) * 128],
                               c.AR0[64:128, b0 + qb * 512 + off:b0 + (qb + 1) * 512],
                               start=True, stop=True, tile_position=(64, 0))
                smm = w.inc("mm", ins)
                if kt == 1 and pend_epi[0] is not None:
                    pend_epi[0]()
                    pend_epi[0] = None
                if kt == 2 and pend_epi[1] is not None:
                    pend_epi[1]()
                    pend_epi[1] = None
                if pend_av is not None:
                    pend_av()
                if jd >= 0:
                    ins = None
                    if w.eng == "V":
                        w.wait("mm", smm)
                        w.e.tensor_tensor(psl[:, 0:128], psl[:, 0:128],
                                          c.cmaskT_s[:], ALU.add)
                        ins = w.e.tensor_tensor(psl[:, 512:640], psl[:, 512:640],
                                                c.cmaskT_s[:], ALU.add)
                    mkv = w.inc("ev", ins)
                else:
                    mkv = None
                ins = None
                if w.eng == "S":
                    w.wait("mm", smm)
                    if mkv is not None:
                        w.wait("ev", mkv)
                    w.res_wait(f"PT{sl}")
                    if jd >= 0:
                        w.e.activation(c.AR1[:, PT[sl]:PT[sl] + width],
                                       psl[:, 0:width], AF.Exp,
                                       bias=c.mbias[:, hp:hp + 1])
                        ins = w.e.activation(c.AR1[:, PT[sl] + 512:PT[sl] + 512 + width],
                                             psl[:, 512:512 + width], AF.Exp,
                                             bias=c.mbias[:, hp:hp + 1])
                    else:
                        ins = w.e.activation(c.AR1[:, PT[sl]:PT[sl] + 1024],
                                             psl[:, 0:1024], AF.Exp,
                                             bias=c.mbias[:, hp:hp + 1])
                exv = w.inc("sc", ins)
                w.res_set(f"pscore{sl}", "sc")

                def av(kt=kt, jd=jd, width=width, sl=sl, exv=exv, vv=vv):
                    first = (kt == 0)
                    ins = None
                    if w.eng == "T":
                        w.wait("sc", exv)
                        if first:
                            w.res_wait("avfree")
                        for h in range(2):
                            bank = avA if h == 0 else avB
                            lhs = vv[:, kt, 65 * h:65 * h + 65]
                            pt_off = PT[sl] + 512 * h
                            if jd < 0:
                                ins = w.e.matmul(bank[0:65, 0:512], lhs,
                                                 c.AR1[:, pt_off:pt_off + 512],
                                                 start=first, stop=False)
                            else:
                                ins = w.e.matmul(bank[0:65, jd * 128:(jd + 1) * 128],
                                                 lhs, c.AR1[:, pt_off:pt_off + 128],
                                                 start=first, stop=True)
                                if jd < 3:
                                    ins = w.e.matmul(bank[0:65, (jd + 1) * 128:512],
                                                     lhs,
                                                     c.AR1[:, pt_off + 128:pt_off + width],
                                                     start=first, stop=False)
                    w.inc("mm", ins)
                    w.res_set(f"PT{sl}", "mm")

                pend_av = av
            pend_av()
            pend_av = None
            av_done = w.cnt["mm"]

            def epi1(av_done=av_done):
                ins = None
                if w.eng == "V":
                    w.wait("mm", av_done)
                    w.res_wait("stgf")
                    w.e.tensor_copy(c.ARF[0:65, 0:512], avA[0:65, 0:512])
                    w.e.tensor_copy(c.ARF[0:65, 512:1024], avB[0:65, 0:512])
                    ins = w.e.drain()
                w.inc("ev", ins)
                w.res_set("avfree", "ev")
                w.tag["rcv"] = w.cnt["ev"]

            def epi2(hp=hp, qb=qb):
                ins = None
                if w.eng == "T":
                    w.wait("ev", w.tag["rcv"])
                    w.res_wait("lbc")
                    w.e.matmul(lbc[0:64, 0:512], c.ones2[64:65, 0:64],
                               c.ARF[64:65, 0:512], start=True, stop=True)
                    ins = w.e.matmul(lbc[64:128, 0:512], c.ones2[64:65, 0:64],
                                     c.ARF[64:65, 512:1024], start=True, stop=True)
                rlv = w.inc("mm", ins)
                ins = None
                if w.eng == "V":
                    w.wait("mm", rlv)
                    w.res_wait("otf")
                    w.e.reciprocal(lbc[:, 0:512], lbc[:, 0:512])
                    w.e.drain()
                    w.e.tensor_tensor(c.AR1[0:64, OTA:OTA + 512],
                                      c.ARF[0:64, 0:512], lbc[0:64, 0:512],
                                      ALU.mult)
                    ins = w.e.tensor_tensor(c.AR1[0:64, OTB:OTB + 512],
                                            c.ARF[0:64, 512:1024],
                                            lbc[64:128, 0:512], ALU.mult)
                mlv = w.inc("ev", ins)
                w.res_set("lbc", "ev")
                w.res_set("stgf", "ev")
                if w.eng == "Y":
                    w.wait("ev", mlv)
                w.dma("st_ot", c.oT_hbm[hp, 0:64, qb * 512:(qb + 1) * 512],
                      c.AR1[0:64, OTA:OTA + 512])
                w.dma("st_ot", c.oT_hbm[hp, 64:128, qb * 512:(qb + 1) * 512],
                      c.AR1[0:64, OTB:OTB + 512])
                w.res_set("otf", "st_ot")

            if pend_epi[0] is not None:
                pend_epi[0]()
            if pend_epi[1] is not None:
                pend_epi[1]()
            pend_epi = [epi1, epi2]
        w.res_set(f"pair{s}", "mm")
        if nxt is not None:
            pair_l = nxt
    pend_epi[0]()
    pend_epi[1]()


def _stageB_proj(w):
    c = w.c
    for ci in range(NC):
        w.dma("wl_out", c.AR1[:, ci * 1024:(ci + 1) * 1024],
              c.wout[ci * 128:(ci + 1) * 128, :])
    w.dma("wl_out", c.row_a[:], c.outb_row)
    wl_v = w.cnt["wl_out"]

    for qt in range(NQT):
        if w.eng == "Y":
            w.res_wait("otb")
        ld1 = w.dma("ld_2c", c.AR3[:, 0:1024].rearrange("p (a t) -> p a t", a=8),
                    c.oT_hbm[:, :, qt * 128:(qt + 1) * 128].rearrange("c p t -> p c t"))
        if w.eng == "Y":
            w.res_wait("x1t2")
        ld2 = w.dma("ld_2c", c.ARF[:, 0:1024], c.x1_hbm[qt * 128:(qt + 1) * 128, :])

        for nb2 in range(2):
            pbk = c.pb[nb2]
            ins = None
            if w.eng == "T":
                w.wait("ld_2c", ld2)
                w.wait("wl_out", wl_v)
                w.res_wait(f"pb{nb2}")
                for ci in range(NC):
                    w.e.matmul(pbk[:],
                               c.AR3[:, ci * 128:(ci + 1) * 128],
                               c.AR1[:, ci * 1024 + nb2 * 512:ci * 1024 + (nb2 + 1) * 512],
                               start=(ci == 0), stop=False)
                ins = w.e.matmul(pbk[:], c.onesbf[:, 0:128],
                                 c.row_a[:, nb2 * 512:(nb2 + 1) * 512],
                                 start=False, stop=True)
            w.inc("mm", ins)
        mm_v = w.cnt["mm"]
        w.res_set("otb", "mm")
        ins = None
        if w.eng == "V":
            w.wait("mm", mm_v)
            w.wait("ld_2c", ld2)
            w.res_wait("tf2")
            w.e.tensor_tensor(c.ARF[:, 1024:1536], c.pb[0][:], c.ARF[:, 0:512], ALU.add)
            w.e.tensor_tensor(c.ARF[:, 1536:2048], c.pb[1][:], c.ARF[:, 512:1024], ALU.add)
            ins = w.e.drain()
        w.inc("ev", ins)
        w.res_set("pb0", "ev")
        w.res_set("pb1", "ev")
        w.res_set("x1t2", "ev")
        _ln_stats(w, c.ARF[:, 1024:2048], 1024)
        ins = None
        if w.eng == "V":
            w.res_wait("x2f")
            w.e.tensor_scalar(c.ARF[:, 2048:3072], c.ARF[:, 1024:2048],
                              c.stat[:, 2:3], c.stat[:, 1:2], ALU.subtract, ALU.mult)
            w.e.drain()
            w.e.tensor_tensor(c.ARF[:, 2048:3072], c.ARF[:, 2048:3072], c.nw_b[:], ALU.mult)
            w.e.drain()
            w.e.tensor_tensor(c.ARF[:, 2048:3072], c.ARF[:, 2048:3072], c.nb_b[:], ALU.add)
            w.e.drain()
            w.res_wait("x2b")
            w.e.tensor_copy(c.AR3[:, 1024:2048], c.ARF[:, 2048:3072])
            ins = w.e.drain()
        x2_v = w.inc("ev", ins)
        w.res_set("tf2", "ev")
        if w.eng == "Y":
            w.wait("ev", x2_v)
        w.dma("st_2c", c.acc_hbm[qt * 128:(qt + 1) * 128, :], c.ARF[:, 2048:3072])
        x2t_ev = _transpose_grp(w, c.AR3, 1024, c.AR3, 2048, 8, ("ev", x2_v),
                                dst_ready_key="x2t")
        w.res_set("x2b", "mm")
        if w.eng == "Y":
            w.wait("ev", x2t_ev)
        w.dma("st_2c",
              c.x2T_hbm[:, :, qt * 128:(qt + 1) * 128].rearrange("c p t -> p c t"),
              c.AR3[:, 2048:3072].rearrange("p (c t) -> p c t", c=8))
        w.res_set("x2t", "st_2c")
        w.res_set("x2f", "st_2c")

        ins = None
        if w.eng == "T":
            w.wait("ev", x2t_ev)
            w.res_wait("par")
            for ci in range(NC):
                ins = w.e.matmul(c.pa[0][:, 0:8],
                                 c.AR3[:, 2048 + ci * 128:2048 + (ci + 1) * 128],
                                 c.centbf[:, ci * 8:(ci + 1) * 8],
                                 start=(ci == 0), stop=(ci == NC - 1))
        lg_v = w.inc("mm", ins)
        ins = None
        if w.eng == "V":
            w.wait("mm", lg_v)
            w.e.tensor_copy(c.stat[:, 8:16], c.pa[0][:, 0:8])
            w.e.drain()
            w.e.reduce_max(c.stat[:, 16:17], c.stat[:, 8:16], axis=AX.X)
            w.e.drain()
            w.e.tensor_scalar(c.stat[:, 17:18], c.stat[:, 16:17], -1.0, None, ALU.mult)
            ins = w.e.drain()
        gv = w.inc("ev", ins)
        w.res_set("par", "ev")
        ins = None
        if w.eng == "S":
            w.wait("ev", gv)
            ins = w.e.activation(c.stat[:, 24:32], c.stat[:, 8:16], AF.Exp,
                                 bias=c.stat[:, 17:18], accum_out=c.stat[:, 32:33])
        sv = w.inc("sc", ins)
        ins = None
        if w.eng == "V":
            w.wait("sc", sv)
            w.e.reciprocal(c.stat[:, 33:34], c.stat[:, 32:33])
            w.e.drain()
            w.e.tensor_scalar(c.stat[:, 24:32], c.stat[:, 24:32], c.stat[:, 33:34],
                              None, ALU.mult)
            w.e.drain()
            w.e.tensor_tensor(c.stat[:, 40:48], c.stat[:, 24:32], c.rbias_b[:], ALU.add)
            w.e.drain()
            w.e.reduce_max(c.stat[:, 48:49], c.stat[:, 40:48], axis=AX.X)
            w.e.drain()
            w.e.tensor_scalar(c.stat[:, 50:58], c.stat[:, 40:48], c.stat[:, 48:49],
                              None, ALU.is_ge)
            w.e.drain()
            w.e.scalar_tensor_tensor(c.stat[:, 50:58], c.stat[:, 50:58], -1e30,
                                     c.stat[:, 40:48], ALU.mult, ALU.add)
            w.e.drain()
            w.e.reduce_max(c.stat[:, 48:49], c.stat[:, 50:58], axis=AX.X)
            w.e.drain()
            w.e.tensor_scalar(c.stat[:, 50:58], c.stat[:, 40:48], c.stat[:, 48:49],
                              None, ALU.is_ge)
            w.e.drain()
            w.e.tensor_tensor(c.stat[:, 50:58], c.stat[:, 50:58], c.stat[:, 24:32],
                              ALU.mult)
            w.e.drain()
            w.e.reduce_sum(c.stat[:, 48:49], c.stat[:, 50:58], axis=AX.X)
            w.e.drain()
            w.e.reciprocal(c.stat[:, 49:50], c.stat[:, 48:49])
            w.e.drain()
            ins = w.e.tensor_scalar(c.gates[:, qt * 8:(qt + 1) * 8], c.stat[:, 50:58],
                                    c.stat[:, 49:50], None, ALU.mult)
        w.inc("ev", ins)
    w.res_set("outw_done", "mm")




def _stageC(w):
    c = w.c

    def load_exp(e):
        sl = (e + 1) % 2
        if w.eng in ("P", "Y") and e == 1:
            w.res_wait("outw_done")
        if w.eng in ("P", "Y") and e >= 2:
            w.res_wait(f"wslot{sl}")
        if e == 0:
            w1s, w2s = c.sw1, c.sw2
        else:
            w1s, w2s = c.rw1[e - 1], c.rw2[e - 1]
        stream = f"wl_w{sl}"
        for ci in range(NC):
            w.dma(stream, c.AR0[:, sl * 16384 + ci * 2048:sl * 16384 + (ci + 1) * 2048],
                  w1s[ci * 128:(ci + 1) * 128, :])
        for fi in range(NFF):
            w.dma(stream, c.AR1[:, sl * 16384 + fi * 1024:sl * 16384 + (fi + 1) * 1024],
                  w2s[fi * 128:(fi + 1) * 128, :])
        return (stream, w.cnt[stream], stream, w.cnt[stream])

    def load_bias(e):
        slot = e % 2
        if w.eng == "P" and e >= 2:
            w.res_wait(f"b2row{slot}")
        b2 = w.dma("wl_b2", (c.row_c if slot == 0 else c.row_c2)[:],
                   c.sb2_row if e == 0 else c.rb2_rows[e - 1:e, :])
        if w.eng == "Y" and e >= 2:
            w.res_wait(f"b1c{slot}")
        b1 = w.dma("ld_b1c", (c.b1c if slot == 0 else c.b1c2)[:],
                   c.sb1c if e == 0 else c.rb1c[e - 1])
        return b2, b1

    wcur = load_exp(0)
    bcur = load_bias(0)
    for e in range(NEXP):
        sl = (e + 1) % 2
        wnxt = load_exp(e + 1) if e + 1 < NEXP else None
        bias_l = bcur
        bnxt = load_bias(e + 1) if e + 1 < NEXP else None
        b1 = c.b1c if e % 2 == 0 else c.b1c2
        b2row = c.row_c if e % 2 == 0 else c.row_c2
        for b in range(NBK):
            bs = b * 512
            xoff = 8192 + (b % 2) * 4096
            if w.eng == "Y":
                w.res_wait(f"x2tb{b % 2}")
                if e == 0:
                    w.wait("st_2c", 32 * (4 * b + 4))
            xst = f"ld_x2t{b % 2}"
            xb_l = w.dma(xst, c.AR2[:, xoff:xoff + 4096].rearrange("p (c t) -> p c t", c=8),
                         c.x2T_hbm[:, :, bs:bs + 512].rearrange("c p t -> p c t"))

            for fi in range(NFF):
                bk = fi % 4
                pbk = c.pb[bk]
                ins = None
                if w.eng == "T":
                    if fi == 0:
                        w.wait(wcur[0], wcur[1])
                        w.wait(wcur[2], wcur[3])
                        w.wait("wl_b2", bias_l[0])
                        w.wait(xst, xb_l)
                        w.res_wait("ht")
                    w.res_wait(f"pb{bk}")
                    for ci in range(NC):
                        ins = w.e.matmul(
                            pbk[:],
                            c.AR0[:, sl * 16384 + ci * 2048 + fi * 128:sl * 16384 + ci * 2048 + fi * 128 + 128],
                            c.AR2[:, xoff + ci * 512:xoff + (ci + 1) * 512],
                            start=(ci == 0), stop=(ci == NC - 1))
                mmv = w.inc("mm", ins)
                ins = None
                if w.eng == "S":
                    w.wait("mm", mmv)
                    if fi == 0:
                        w.wait("ld_b1c", bias_l[1])
                    ins = w.e.activation(c.AR2[:, fi * 512:(fi + 1) * 512], pbk[:],
                                         AF.Gelu, bias=b1[:, fi:fi + 1])
                w.inc("sc", ins)
                w.res_set(f"pb{bk}", "sc")
            h_v = w.cnt["sc"]
            for tt in range(4):
                for nb2 in range(2):
                    bk = (tt * 2 + nb2) % 4
                    pbk = c.pb[bk]
                    ins = None
                    if w.eng == "T":
                        if tt == 0 and nb2 == 0:
                            w.wait("sc", h_v)
                        w.res_wait(f"pb{bk}")
                        for fi in range(NFF):
                            w.e.matmul(
                                pbk[:],
                                c.AR2[:, fi * 512 + tt * 128:fi * 512 + tt * 128 + 128],
                                c.AR1[:, sl * 16384 + fi * 1024 + nb2 * 512:sl * 16384 + fi * 1024 + (nb2 + 1) * 512],
                                start=(fi == 0), stop=False)
                        ins = w.e.matmul(pbk[:], c.onesbf[:, 0:128],
                                         b2row[:, nb2 * 512:(nb2 + 1) * 512],
                                         start=False, stop=True)
                    mmv = w.inc("mm", ins)
                    ins = None
                    if w.eng == "V":
                        w.wait("mm", mmv)
                        if e == 0 and b == 0 and tt == 0 and nb2 == 0:
                            w.wait("ld_2c", w.cnt["ld_2c"])
                        if nb2 == 0 and tt % 2 == 0:
                            w.res_wait("ybuf2" if tt == 0 else "ybuf")
                        dst = c.ARF[:, (tt % 2) * 1024 + nb2 * 512:(tt % 2) * 1024 + (nb2 + 1) * 512]
                        if e == 0:
                            ins = w.e.tensor_copy(dst, pbk[:])
                        else:
                            gcol = c.gates[:, (b * 4 + tt) * 8 + (e - 1):(b * 4 + tt) * 8 + e]
                            ins = w.e.tensor_scalar(dst, pbk[:], gcol, None, ALU.mult)
                    yev = w.inc("ev", ins)
                    w.res_set(f"pb{bk}", "ev")
                if tt == 1:
                    seq = (e * NBK + b) * 2
                    if w.eng == "P":
                        w.wait("ev", yev)
                        w.wait("ga", 16 * seq)
                    w.dma("ga",
                          c.acc_hbm[bs:bs + 256, :].rearrange("(a p) n -> p a n", p=128),
                          c.ARF[:, 0:2048].rearrange("p (a n) -> p a n", a=2),
                          accum=ALU.add)
                    w.res_set("ybuf", "ga")
            y_v = w.cnt["ev"]
            seq = (e * NBK + b) * 2 + 1
            if w.eng == "P":
                w.wait("ev", y_v)
                w.wait("ga", 16 * seq)
            w.dma("ga",
                  c.acc_hbm[bs + 256:bs + 512, :].rearrange("(a p) n -> p a n", p=128),
                  c.ARF[:, 0:2048].rearrange("p (a n) -> p a n", a=2), accum=ALU.add)
            w.res_set("ybuf2", "ga")
            w.res_set(f"x2tb{b % 2}", "mm")
            w.res_set("ht", "mm")
        w.res_set(f"wslot{sl}", "mm")
        w.res_set(f"b2row{e % 2}", "mm")
        w.res_set(f"b1c{e % 2}", "sc")
        if wnxt is not None:
            wcur = wnxt
            bcur = bnxt




def _stageD(w):
    c = w.c
    ylv = w.dma("ld_pro", c.row_f[:], c.n2w_row)
    if w.eng == "T":
        w.wait("ld_pro", ylv)
    _bcast_row(w, c.row_f, c.nw_b, E, c.pb[0], "pb0")
    w.res_set("rowf", "mm")
    if w.eng == "Y":
        w.res_wait("rowf")
    nb_l = w.dma("ld_pro", c.row_f[:], c.n2b_row)
    if w.eng == "T":
        w.wait("ld_pro", nb_l)
    _bcast_row(w, c.row_f, c.nb_b, E, c.pb[1], "pb1")
    bc_v = w.cnt["ev"]

    for qt in range(NQT):
        if w.eng == "Y":
            w.res_wait("acct")
        ld1 = w.dma("ld_acc", c.ARF[:, 0:1024], c.acc_hbm[qt * 128:(qt + 1) * 128, :])
        if w.eng == "Y":
            w.res_wait("x1t3")
        ld2 = w.dma("ld_acc", c.ARF[:, 1024:2048], c.x1_hbm[qt * 128:(qt + 1) * 128, :])
        if w.eng == "V":
            w.wait("ld_acc", ld2)
            w.wait("ev", bc_v)
            w.res_wait("acct_w")
        _ln_stats(w, c.ARF[:, 0:1024], 1024)
        ins = None
        if w.eng == "V":
            w.res_wait("tf3")
            w.e.tensor_scalar(c.ARF[:, 2048:3072], c.ARF[:, 0:1024],
                              c.stat[:, 2:3], c.stat[:, 1:2], ALU.subtract, ALU.mult)
            w.e.drain()
            w.e.tensor_tensor(c.ARF[:, 2048:3072], c.ARF[:, 2048:3072], c.nw_b[:], ALU.mult)
            w.e.drain()
            w.e.tensor_tensor(c.ARF[:, 2048:3072], c.ARF[:, 2048:3072], c.nb_b[:], ALU.add)
            w.e.drain()
            w.e.tensor_tensor(c.ARF[:, 2048:3072], c.ARF[:, 2048:3072],
                              c.ARF[:, 1024:2048], ALU.add)
            ins = w.e.drain()
        w.inc("ev", ins)
        w.res_set("acct", "ev")
        w.res_set("x1t3", "ev")
        _ln_stats(w, c.ARF[:, 2048:3072], 1024)
        ins = None
        if w.eng == "V":
            ins = w.e.tensor_scalar(c.ARF[:, 0:1024], c.ARF[:, 2048:3072],
                                    c.stat[:, 2:3], c.stat[:, 1:2],
                                    ALU.subtract, ALU.mult)
        ov = w.inc("ev", ins)
        w.res_set("tf3", "ev")
        w.res_set("acct_w", "st_out", w.cnt["st_out"] + 16)
        if w.eng == "Y":
            w.wait("ev", ov)
        w.dma("st_out", c.out[qt * 128:(qt + 1) * 128, :], c.ARF[:, 0:1024])
        w.res_set("acct", "st_out")
    if w.eng == "Y":
        w.wait("st_out", w.cnt["st_out"])




def _hmaskc():
    m = np.zeros((128, 128), dtype=np.float32)
    for ci in range(8):
        m[0:64, ci * 16 + 2 * ci] = 1.0
        m[64:128, ci * 16 + 2 * ci + 1] = 1.0
    return m


def _stage_inputs(inputs):
    g = lambda k: np.asarray(inputs[k], dtype=np.float32)
    x = g("L0_attn_x")
    qkv_b = g("L0_ffn_qkv_b")
    shared = {
        "wqkv1": np.ascontiguousarray(g("L0_attn_W_qkv")),
        "wqkv2": np.ascontiguousarray(g("L0_ffn_qkv_w")),
        "wout": np.ascontiguousarray(g("L0_ffn_out_w")),
        "sw1": np.ascontiguousarray(g("L0_ffn_sw1")[0]),
        "sw2": np.ascontiguousarray(g("L0_ffn_sw2")[0]),
        "rw1": np.ascontiguousarray(g("L0_ffn_rw1")),
        "rw2": np.ascontiguousarray(g("L0_ffn_rw2")),
        "qkcols": np.ascontiguousarray(
            np.concatenate([qkv_b[0:E].reshape(NC, 128).T,
                            qkv_b[E:2 * E].reshape(NC, 128).T], axis=1)),
        "vb_row": np.ascontiguousarray(qkv_b[2 * E:3 * E].reshape(1, E)),
        "outb_row": np.ascontiguousarray(g("L0_ffn_out_b").reshape(1, E)),
        "n1w_row": np.ascontiguousarray(g("L0_ffn_norm1_w").reshape(1, E)),
        "n1b_row": np.ascontiguousarray(g("L0_ffn_norm1_b").reshape(1, E)),
        "n2w_row": np.ascontiguousarray(g("L0_ffn_norm2_w").reshape(1, E)),
        "n2b_row": np.ascontiguousarray(g("L0_ffn_norm2_b").reshape(1, E)),
        "sb1c": np.ascontiguousarray(g("L0_ffn_sb1")[0].reshape(NFF, 128).T),
        "sb2_row": np.ascontiguousarray(g("L0_ffn_sb2")[0].reshape(1, E)),
        "rb1c": np.ascontiguousarray(g("L0_ffn_rb1").reshape(8, NFF, 128).transpose(0, 2, 1)),
        "rb2_rows": np.ascontiguousarray(g("L0_ffn_rb2")),
        "centT": np.ascontiguousarray(g("L0_ffn_centroid").T.reshape(NC, 128, 8)),
        "rbias_row": np.ascontiguousarray(g("L0_ffn_route_bias").reshape(1, 8)),
        "ident": np.eye(128, dtype=np.float32),
        "cmask": np.triu(np.full((128, 128), -1e9, dtype=np.float32), k=1),
        "cmaskT": np.tril(np.full((128, 128), -1e9, dtype=np.float32), k=-1),
        "hmaskc": _hmaskc(),
    }
    maps = []
    for b in range(8):
        m = dict(shared)
        m["x0"] = np.ascontiguousarray(x[b])
        m["x0T"] = np.ascontiguousarray(x[b].T)
        maps.append(m)
    return maps


def _run(inputs, taps=(), nstages=8, apart=9):
    global _NSTAGES, _APART
    _NSTAGES = nstages
    _APART = apart
    nc, _ = _build(taps=taps)
    _NSTAGES = 8
    maps = _stage_inputs(inputs)
    return run_bass_kernel_spmd(nc, maps, core_ids=list(range(8)))


def kernel(**inputs):
    res = _run(inputs)
    return np.stack([r["out"] for r in res.results], axis=0)
